# version 06
# baseline (speedup 1.0000x reference)
import numpy as np
import ml_dtypes
import concourse.bass as bass
import concourse.mybir as mybir
from concourse.bass_utils import run_bass_kernel_spmd
from contextlib import ExitStack

F32 = mybir.dt.float32
BF16 = mybir.dt.bfloat16
U32 = mybir.dt.uint32
AF = mybir.ActivationFunctionType
OP = mybir.AluOpType
AX = mybir.AxisListType

B, L, D = 4, 8192, 1024
TM = 4096
NE, DHID = 8, 2048
CAP = 5120
NFFT = 16384
GELU_C = float(np.sqrt(2.0 / np.pi))
EPS = 1e-5
NBEF = 24576
G = 4
NSG = D // G

bfnp = ml_dtypes.bfloat16

CONST_SPECS = [("DrDi", [128, 256]), ("Dr", [128, 128]), ("Di", [128, 128]),
               ("negDi", [128, 128]), ("DbrDbi", [128, 256]),
               ("negDbiDbr", [128, 256]), ("DbrS", [128, 32]),
               ("negDbiS", [128, 32]),
               ("Twfr4", [128, 512]), ("Twfi4", [128, 512]),
               ("Twbr4", [128, 512]), ("Twbi4", [128, 512]),
               ("decay4", [64, 512])]


def _bf(x):
    return np.ascontiguousarray(np.asarray(x).astype(bfnp))


def _f32(x):
    return np.ascontiguousarray(np.asarray(x, dtype=np.float32))


def make_host_constants(myh):
    a = np.arange(128)
    Dm = np.exp(-2j * np.pi * np.outer(a, a) / 128)
    T = np.exp(-2j * np.pi * np.outer(a, a) / NFFT)
    Tb = np.exp(+2j * np.pi * np.outer(a, a) / NFFT) / NFFT
    Dr, Di = Dm.real, Dm.imag
    c = {}
    c["DrDi"] = np.concatenate([Dr, Di], 1)
    c["Dr"], c["Di"], c["negDi"] = Dr, Di, -Di
    c["DbrDbi"] = np.concatenate([Dr, -Di], 1)
    c["negDbiDbr"] = np.concatenate([Di, Dr], 1)
    c["DbrS"] = Dr[:, 32 * myh:32 * myh + 32]
    c["negDbiS"] = Di[:, 32 * myh:32 * myh + 32]
    c["Twfr4"] = np.concatenate([T.real] * G, 1)
    c["Twfi4"] = np.concatenate([T.imag] * G, 1)
    c["Twbr4"] = np.concatenate([Tb.real] * G, 1)
    c["Twbi4"] = np.concatenate([Tb.imag] * G, 1)
    t = np.arange(L, dtype=np.float64).reshape(64, 128)
    c["decay4"] = np.concatenate([np.exp(-0.01 * t)] * G, 1)
    pos = np.arange(L, dtype=np.float32)[:, None]
    div = np.exp(np.arange(0, 64, 2, dtype=np.float32) * -(np.log(10000.0) / 64))
    pe = np.zeros((L, 64), np.float32)
    pe[:, 0::2] = np.sin(pos * div)
    pe[:, 1::2] = np.cos(pos * div)
    c["posT"] = pe.T
    c["ident"] = np.eye(128, dtype=np.float32)
    c["iota8"] = np.tile(np.arange(8, dtype=np.float32), (128, 1))
    c["ones_col"] = np.ones((128, 1), np.float32)
    c["ones_row"] = np.ones((1, 512), np.float32)
    return c


class Ctr:
    def __init__(self, handle):
        self.h = handle
        self.n = 0

    def inc(self, k=1):
        self.n += k
        return self.n


def build_program():
    nc = bass.Bass("TRN2", target_bir_lowering=False)
    dram_in = {}

    def din(name, shape, dtype=F32):
        dram_in[name] = nc.dram_tensor(name, shape, dtype, kind="ExternalInput").ap()
        return dram_in[name]

    x_mine = din("x_mine", [TM, D])
    xT_true = din("xT_true", [D, L])
    xT_part = din("xT_part", [D, TM])
    xT_bef = din("xT_bef", [D, NBEF])
    val_bef = din("val_bef", [NBEF // 512, 128, 4])
    val_part = din("val_part", [TM // 512, 128, 4])
    Wu = din("Wu", [D, D]); Wv = din("Wv", [D, D]); Wo = din("Wo", [D, D])
    W1 = din("W1", [64, 64]); W2 = din("W2", [64, 64]); W3 = din("W3", [64, D])
    bu_c = din("bu_c", [128, 8]); bv_c = din("bv_c", [128, 8]); b3_c = din("b3_c", [128, 8])
    bo = din("bo", [1, D])
    b1c = din("b1c", [64, 1]); b2c = din("b2c", [64, 1])
    wrT = din("wrT", [D, NE])
    wr9_d = din("wr9", [D, NE + 2])
    wgT = din("wgT", [D, DHID])
    wdT = din("wdT", [NE, DHID, D])
    for k, shp in CONST_SPECS:
        din("c_" + k, shp, BF16)
    posT_d = din("c_posT", [64, L])
    ident_d = din("c_ident", [128, 128])
    iota8_d = din("c_iota8", [128, 8])
    ones_col_d = din("c_ones_col", [128, 1])
    ones_row_d = din("c_ones_row", [1, 512])
    ones8_d = din("c_ones8", [128, 8])
    epsc_d = din("c_epsc", [128, 1])

    out_d = nc.dram_tensor("out", [TM, D], F32, kind="ExternalOutput").ap()
    DBG = False
    if DBG:
        dbg_outs = {
            "d_vT": nc.dram_tensor("d_vT", [D, L], BF16, kind="ExternalOutput").ap(),
            "d_hT": nc.dram_tensor("d_hT", [D, L], BF16, kind="ExternalOutput").ap(),
            "d_vc": nc.dram_tensor("d_vc", [D, TM], BF16, kind="ExternalOutput").ap(),
            "d_s": nc.dram_tensor("d_s", [TM, D], F32, kind="ExternalOutput").ap(),
            "d_acc": nc.dram_tensor("d_acc", [TM, D], F32, kind="ExternalOutput").ap(),
            "d_comb": nc.dram_tensor("d_comb", [128, 32, 8], F32, kind="ExternalOutput").ap(),
            "d_g12": nc.dram_tensor("d_g12", [128, 32, 2], F32, kind="ExternalOutput").ap(),
            "d_lgm": nc.dram_tensor("d_lgm", [128, 32, 8], F32, kind="ExternalOutput").ap(),
            "d_m1s": nc.dram_tensor("d_m1s", [128, 32, 8], F32, kind="ExternalOutput").ap(),
            "d_m2s": nc.dram_tensor("d_m2s", [128, 32, 8], F32, kind="ExternalOutput").ap(),
            "d_kq": nc.dram_tensor("d_kq", [128, 32, 8], F32, kind="ExternalOutput").ap(),
            "d_keep": nc.dram_tensor("d_keep", [8, 4096], F32, kind="ExternalOutput").ap(),
            "d_totB": nc.dram_tensor("d_totB", [8, 1], F32, kind="ExternalOutput").ap(),
            "d_mx": nc.dram_tensor("d_mx", [128, 32, 8], F32, kind="ExternalOutput").ap(),
            "d_t9s": nc.dram_tensor("d_t9s", [128, 10], F32, kind="ExternalOutput").ap(),
            "d_mk1": nc.dram_tensor("d_mk1", [128, 8], F32, kind="ExternalOutput").ap(),
            "d_t8b": nc.dram_tensor("d_t8b", [128, 8], F32, kind="ExternalOutput").ap(),
            "d_totrow": nc.dram_tensor("d_totrow", [1, 8], F32, kind="ExternalOutput").ap(),
            "d_t8snap": nc.dram_tensor("d_t8snap", [128, 8], F32, kind="ExternalOutput").ap(),
            "d_mksnap": nc.dram_tensor("d_mksnap", [128, 8], F32, kind="ExternalOutput").ap(),
            "d_t9snap": nc.dram_tensor("d_t9snap", [128, 10], F32, kind="ExternalOutput").ap(),
            "d_t8pre": nc.dram_tensor("d_t8pre", [128, 8], F32, kind="ExternalOutput").ap(),
            "d_csb": nc.dram_tensor("d_csb", [128, 8], F32, kind="ExternalOutput").ap(),
        }

    def dint(name, shape, dtype):
        return nc.dram_tensor(name, shape, dtype, kind="Internal").ap()

    Wu_bf = dint("Wu_bf", [D, D], BF16)
    Wv_bf = dint("Wv_bf", [D, D], BF16)
    Wo_bf = dint("Wo_bf", [D, D], BF16)
    W1_bf = dint("W1_bf", [64, 64], BF16)
    W2_bf = dint("W2_bf", [64, 64], BF16)
    W3_bf = dint("W3_bf", [64, D], BF16)
    wgT_bf = dint("wgT_bf", [D, DHID], BF16)
    wdT_bf = dint("wdT_bf", [NE, DHID, D], BF16)
    vT_d = dint("vT_d", [D, L], BF16)
    hT_d = dint("hT_d", [D, L], BF16)
    vc_d = dint("vc_d", [D, TM], BF16)
    s_d = dint("s_d", [TM, D], F32)

    ctx = ExitStack()

    def sbuf(name, shape, dtype):
        return ctx.enter_context(nc.sbuf_tensor(name, shape, dtype))

    sb = {}
    for k, shp in CONST_SPECS:
        sb[k] = sbuf("k_" + k, shp, BF16)
    ident = sbuf("k_ident", [128, 128], F32)
    iota8 = sbuf("k_iota8", [128, 8], F32)
    ones_col_bf = sbuf("k_ones_col_bf", [128, 1], BF16)
    ones_row_bf = sbuf("k_ones_row_bf", [1, 512], BF16)
    ones_col = sbuf("k_ones_col", [128, 1], F32)
    ones8bf = sbuf("k_ones8bf", [128, 8], BF16)
    epsc = sbuf("k_epsc", [128, 1], F32)
    xnT = sbuf("xnT", [128, 8, TM], BF16)
    comb = sbuf("comb", [128, 32, 8], F32)
    buc = sbuf("buc", [128, 8], F32)
    bvc = sbuf("bvc", [128, 8], F32)
    b3c = sbuf("b3c", [128, 8], F32)
    wr32 = sbuf("wr32", [128, 8, 8], F32)
    wrbf = sbuf("wrbf", [128, 8, 8], BF16)
    t8a = sbuf("t8a", [128, 32], F32)
    mk4 = sbuf("mk4", [128, 4, 8], F32)
    m8a = sbuf("m8a", [128, 8], F32)
    t8b = sbuf("t8b", [128, 8], F32)
    c1 = sbuf("c1", [128, 1], F32)
    c2 = sbuf("c2", [128, 1], F32)
    c3 = sbuf("c3", [128, 1], F32)
    c4 = sbuf("c4", [128, 1], F32)
    valc = sbuf("valc", [128, 4], F32)
    mi_u = sbuf("mi_u", [128, 8], U32)
    bn6 = sbuf("bn6", [128, 12], F32)
    mv2 = sbuf("mv2", [128, 2], F32)
    vspc = sbuf("vspc", [128, 1], F32)
    sspc = sbuf("sspc", [128, 1], F32)
    poolA = ExitStack()

    def sbufA(name, shape, dtype):
        return poolA.enter_context(nc.sbuf_tensor(name, shape, dtype))

    xc = sbufA("xc", [128, 8, 512], BF16)
    sq8 = sbufA("sq8", [128, 8, 512], BF16)
    rstd_b = sbufA("rstd_b", [128, 512], F32)
    nmr_b = sbufA("nmr_b", [128, 512], F32)
    row1 = sbufA("row1", [1, 512], F32)
    row2 = sbufA("row2", [1, 512], F32)
    row3 = sbufA("row3", [1, 512], F32)
    lgT = sbufA("lgT", [8, 512], F32)
    csr_row = sbufA("csr_row", [1, 8], F32)
    ones_row_f = sbufA("ones_row_f", [1, 128], F32)
    vstage = [sbufA(f"vstage{p}", [128, 512], BF16) for p in range(2)]
    bo_row = sbufA("bo_row2", [1, D], BF16)

    pA = ctx.enter_context(nc.psum_tensor("pA", [128, 512], F32))
    pB = ctx.enter_context(nc.psum_tensor("pB", [128, 512], F32))
    pC = ctx.enter_context(nc.psum_tensor("pC", [128, 512], F32))
    pD = ctx.enter_context(nc.psum_tensor("pD", [128, 512], F32))
    pT = ctx.enter_context(nc.psum_tensor("pT", [128, 1024], F32))
    pF = ctx.enter_context(nc.psum_tensor("pF", [32, 512], F32))

    sems = {}
    for s in ["ldc", "cast", "xl", "vl", "wl",
              "mmA", "mmB", "mmIA", "mmF", "evA", "evB", "evIA", "evF",
              "twf", "pw", "twi", "xld",
              "tmm", "sev", "vve", "gps", "st1", "st2", "st3", "st4", "st5",
              "wlg", "ph"]:
        sems[s] = Ctr(ctx.enter_context(nc.semaphore("sm_" + s)))

    block = ctx.enter_context(nc.Block())
    nbar = [0]

    SYNC_DMA_SEMS = ["ldc", "xl", "wl", "st5", "xld"]
    GP_DMA_SEMS = ["cast", "vl", "wlg", "st1", "st2", "st3", "st4"]

    def barrier():
        nbar[0] += 1
        target = 5 * nbar[0]

        def tbody(e):
            e.nop().then_inc(sems["ph"].h, 1)
            e.wait_ge(sems["ph"].h, target)
        block.tensor(tbody)

        def sbody(e):
            e.drain()
            e.nop().then_inc(sems["ph"].h, 1)
            e.wait_ge(sems["ph"].h, target)
        block.scalar(sbody)

        def vbody(e):
            e.drain()
            e.nop().then_inc(sems["ph"].h, 1)
            e.wait_ge(sems["ph"].h, target)
        block.vector(vbody)

        def sybody(e):
            for s in SYNC_DMA_SEMS:
                if sems[s].n:
                    e.wait_ge(sems[s].h, sems[s].n)
            e.nop().then_inc(sems["ph"].h, 1)
            e.wait_ge(sems["ph"].h, target)
        block.sync(sybody)

        def gbody(e):
            for s in GP_DMA_SEMS:
                if sems[s].n:
                    e.wait_ge(sems[s].h, sems[s].n)
            e.nop().then_inc(sems["ph"].h, 1)
            e.wait_ge(sems["ph"].h, target)
        block.gpsimd(gbody)

    def snap():
        return {k: v.n for k, v in sems.items()}

    @block.sync
    def _(sy):
        n = 0
        for k, _shp in CONST_SPECS:
            sy.dma_start(out=sb[k][:], in_=dram_in["c_" + k]).then_inc(sems["ldc"].h, 16)
            n += 16
        for dst, src in [(ident, ident_d), (iota8, iota8_d), (ones_col, ones_col_d),
                         (epsc, epsc_d)]:
            sy.dma_start(out=dst[:], in_=src).then_inc(sems["ldc"].h, 16)
            n += 16
        for dst, src in [(buc, bu_c), (bvc, bv_c), (b3c, b3_c)]:
            sy.dma_start(out=dst[:], in_=src).then_inc(sems["ldc"].h, 16)
            n += 16
        sy.dma_start(out=wr32[:], in_=wrT.rearrange("(a p) e -> p a e", p=128)
                     ).then_inc(sems["ldc"].h, 16)
        n += 16
        sy.dma_start(out=ones_row_f[:], in_=ones_row_d[0:1, 0:128]).then_inc(sems["ldc"].h, 16)
        n += 16
        sy.wait_ge(sems["ldc"].h, n)
    sems["ldc"].inc(16 * (len(CONST_SPECS) + 9))

    @block.gpsimd
    def _(g):
        n = 0
        for src, dst in [(Wu, Wu_bf), (Wv, Wv_bf), (Wo, Wo_bf), (wgT, wgT_bf),
                         (W1, W1_bf), (W2, W2_bf), (W3, W3_bf)]:
            g.dma_start(out=dst, in_=src).then_inc(sems["cast"].h, 16)
            n += 16
        for e in range(NE):
            g.dma_start(out=wdT_bf[e], in_=wdT[e]).then_inc(sems["cast"].h, 16)
            n += 16
        for dst, src in [(ones_col_bf, ones_col_d), (ones_row_bf, ones_row_d),
                         (bo_row, bo)]:
            g.dma_start(out=dst[:], in_=src).then_inc(sems["cast"].h, 16)
            n += 16
        g.dma_start(out=wrbf[:], in_=wrT.rearrange("(a p) e -> p a e", p=128)
                    ).then_inc(sems["cast"].h, 16)
        n += 16
        g.dma_start(out=ones8bf[:], in_=ones8_d).then_inc(sems["cast"].h, 16)
        n += 16
        g.wait_ge(sems["cast"].h, n)
    sems["cast"].inc(16 * 20)

    barrier()

    with ExitStack() as c13:
        lgm = c13.enter_context(nc.sbuf_tensor("lgm", [128, 32, 8], F32))
        m1s = c13.enter_context(nc.sbuf_tensor("m1s", [128, 32, 8], F32))
        m2s = c13.enter_context(nc.sbuf_tensor("m2s", [128, 32, 8], F32))
        msum = c13.enter_context(nc.sbuf_tensor("msum", [128, 32, 8], F32))
        mx_all = c13.enter_context(nc.sbuf_tensor("mx_all", [128, 32, 8], F32))
        g12 = c13.enter_context(nc.sbuf_tensor("g12", [128, 32, 2], F32))
        r32a = c13.enter_context(nc.sbuf_tensor("r32a", [128, 32], F32))
        r32b = c13.enter_context(nc.sbuf_tensor("r32b", [128, 32], F32))
        r32c = c13.enter_context(nc.sbuf_tensor("r32c", [128, 32], F32))
        r32d = c13.enter_context(nc.sbuf_tensor("r32d", [128, 32], F32))
        maskT = c13.enter_context(nc.sbuf_tensor("maskT", [8, 6144], F32))
        poscum = c13.enter_context(nc.sbuf_tensor("poscum", [8, 6144], F32))
        keepT = c13.enter_context(nc.sbuf_tensor("keepT", [8, 4096], F32))
        totB = c13.enter_context(nc.sbuf_tensor("totB", [8, 1], F32))
        kq = c13.enter_context(nc.sbuf_tensor("kq", [128, 32, 8], F32))
        xtile = c13.enter_context(nc.sbuf_tensor("xtile", [128, 1024], F32))
        xn32 = c13.enter_context(nc.sbuf_tensor("xn32", [128, 1024], F32))
        xnT32 = c13.enter_context(nc.sbuf_tensor("xnT32", [128, 8, 128], F32))
        xst0 = c13.enter_context(nc.sbuf_tensor("xst0", [128, 8, 128], F32))
        xst1 = c13.enter_context(nc.sbuf_tensor("xst1", [128, 8, 128], F32))
        wr9s = c13.enter_context(nc.sbuf_tensor("wr9s", [128, 8, 10], F32))
        cs_b = c13.enter_context(nc.sbuf_tensor("cs_b", [128, 8], F32))
        mk1 = c13.enter_context(nc.sbuf_tensor("mk1", [128, 8], F32))
        t8snap = c13.enter_context(nc.sbuf_tensor("t8snap", [128, 8], F32))
        mksnap = c13.enter_context(nc.sbuf_tensor("mksnap", [128, 8], F32))
        t9snap = c13.enter_context(nc.sbuf_tensor("t9snap", [128, 16], F32))
        t8pre = c13.enter_context(nc.sbuf_tensor("t8pre", [128, 8], F32))
        nd128 = c13.enter_context(nc.sbuf_tensor("nd128", [128, 128], BF16))
        t9s = c13.enter_context(nc.sbuf_tensor("t9s", [128, 10], F32))
        totrow = c13.enter_context(nc.sbuf_tensor("totrow", [1, 8], F32))
        trow = c13.enter_context(nc.sbuf_tensor("trow", [1, 8], F32))

        b0 = snap()

        @block.vector
        def _(v):
            v.memset(maskT[:], 0.0)
            v.memset(poscum[:], 0.0)
            v.memset(nd128[:], -1.0 / D)
            for T in range(32):
                v.wait_ge(sems["xl"].h, b0["xl"] + 16 * (T + 1))
                if T > 0:
                    v.wait_ge(sems["sev"].h, b0["sev"] + 3 * (T - 1) + 1)
                v.bn_stats(bn6[:, 0:6], xtile[:, 0:512])
                v.bn_stats(bn6[:, 6:12], xtile[:, 512:1024])
                v.tensor_copy(vspc[:], epsc[:])
                v.tensor_copy(vspc[:], epsc[:])
                v.bn_aggr(mv2[:], bn6[:]).then_inc(sems["vve"].h, 1)

        @block.sync
        def _(sy):
            for T in range(32):
                if T > 0:
                    sy.wait_ge(sems["sev"].h, b0["sev"] + 3 * (T - 1) + 1)
                sy.dma_start(out=xtile[:], in_=x_mine[T * 128:(T + 1) * 128, :]
                             ).then_inc(sems["xl"].h, 16)

        @block.scalar
        def _(s):
            for T in range(32):
                s.wait_ge(sems["vve"].h, b0["vve"] + T + 1)
                s.activation(out=c1[:], in_=mv2[:, 1:2], func=AF.Ln, bias=epsc[:, 0:1], scale=1.0)
                s.activation(out=c2[:], in_=c1[:], func=AF.Exp, scale=-0.5)
                s.mul(c3[:], mv2[:, 0:1], -1.0)
                s.activation(out=c4[:], in_=c3[:], func=AF.Identity, scale=c2[:, 0:1])
                s.mul(sspc[:], epsc[:], 1.0)
                s.mul(sspc[:], epsc[:], 1.0)
                s.activation(out=xn32[:], in_=xtile[:], func=AF.Identity,
                             scale=c2[:, 0:1], bias=c4[:, 0:1]
                             ).then_inc(sems["sev"].h, 1)
                s.wait_ge(sems["tmm"].h, b0["tmm"] + 2 * T + 1)
                s.copy(out=xnT32[:], in_=pT[:].rearrange("p (a q) -> p a q", q=128))
                s.copy(out=xnT[:, :, T * 128:(T + 1) * 128],
                       in_=pT[:].rearrange("p (a q) -> p a q", q=128)
                       ).then_inc(sems["sev"].h, 1)
                s.wait_ge(sems["tmm"].h, b0["tmm"] + 2 * T + 2)
                s.copy(out=lgm[:, T, :], in_=pA[:, 0:8]).then_inc(sems["sev"].h, 1)

        @block.tensor
        def _(t):
            for T in range(32):
                t.wait_ge(sems["sev"].h, b0["sev"] + 3 * T + 1)
                for cb in range(8):
                    mm = t.transpose(pT[:, cb * 128:(cb + 1) * 128],
                                     xn32[:, cb * 128:(cb + 1) * 128], ident[:])
                mm.then_inc(sems["tmm"].h, 1)
                t.wait_ge(sems["sev"].h, b0["sev"] + 3 * T + 2)
                for cb in range(8):
                    mm = t.matmul(pA[:, 0:8], xnT32[:, cb, :], wr32[:, cb, :],
                                  start=(cb == 0), stop=(cb == 7))
                mm.then_inc(sems["tmm"].h, 1)

        sems["xl"].inc(16 * 32)
        sems["vve"].inc(32)
        sems["sev"].inc(3 * 32)
        sems["tmm"].inc(2 * 32)
        barrier()

        b1b = snap()

        @block.vector
        def _(v):
            for T in range(32):
                v.tensor_reduce(r32a[:, T:T + 1], lgm[:, T, :], axis=AX.X, op=OP.max)
            for T in range(32):
                v.tensor_scalar(out=lgm[:, T, :], in0=lgm[:, T, :],
                                scalar1=r32a[:, T:T + 1], scalar2=None, op0=OP.subtract)
            for T in range(32):
                v.max_with_indices(out_max=mx_all[:, T, :], out_indices=mi_u[:],
                                   in_=lgm[:, T, :])
                v.tensor_scalar(out=m1s[:, T, :], in0=lgm[:, T, :],
                                scalar1=mx_all[:, T, 0:1], scalar2=None,
                                op0=OP.is_equal)
                v.tensor_scalar(out=m2s[:, T, :], in0=lgm[:, T, :],
                                scalar1=mx_all[:, T, 1:2], scalar2=None,
                                op0=OP.is_equal)
            v.tensor_copy(vspc[:], epsc[:])
            v.tensor_copy(vspc[:], epsc[:])
            v.tensor_add(msum[:], m1s[:], m2s[:]).then_inc(sems["vve"].h, 1)
            v.wait_ge(sems["sev"].h, b1b["sev"] + 1)
            for T in range(32):
                red = v.tensor_reduce(r32b[:, T:T + 1], lgm[:, T, :], axis=AX.X,
                                      op=OP.add)
            red.then_inc(sems["vve"].h, 1)
            v.wait_ge(sems["sev"].h, b1b["sev"] + 2)
            v.tensor_mul(r32d[:], r32d[:], r32c[:])
            v.tensor_copy(g12[:, :, 0], r32c[:])
            v.tensor_copy(g12[:, :, 1], r32d[:]).then_inc(sems["vve"].h, 1)

        @block.scalar
        def _(s):
            s.wait_ge(sems["vve"].h, b1b["vve"] + 1)
            s.activation(out=lgm[:], in_=lgm[:], func=AF.Exp)
            for T in range(32):
                act = s.activation(out=r32d[:, T:T + 1], in_=mx_all[:, T, 1:2],
                                   func=AF.Exp)
            act.then_inc(sems["sev"].h, 1)
            s.wait_ge(sems["vve"].h, b1b["vve"] + 2)
            s.activation(out=r32c[:], in_=r32b[:], func=AF.Ln)
            s.activation(out=r32c[:], in_=r32c[:], func=AF.Exp, scale=-1.0
                         ).then_inc(sems["sev"].h, 1)
            for bb in range(4):
                s.wait_ge(sems["tmm"].h, b1b["tmm"] + bb + 1)
                s.copy(out=maskT[:, 2048 + bb * 1024:2048 + (bb + 1) * 1024],
                       in_=pT[0:8, :]).then_inc(sems["sev"].h, 1)

        @block.tensor
        def _(t):
            t.wait_ge(sems["vve"].h, b1b["vve"] + 1)
            for bb in range(4):
                if bb > 0:
                    t.wait_ge(sems["sev"].h, b1b["sev"] + 2 + bb)
                for q in range(8):
                    T = bb * 8 + q
                    mm = t.transpose(pT[0:8, q * 128:(q + 1) * 128], msum[:, T, :],
                                     ident[:])
                mm.then_inc(sems["tmm"].h, 1)

        sems["vve"].inc(3)
        sems["sev"].inc(6)
        sems["tmm"].inc(4)
        barrier()

        b2 = snap()
        NST = (TM + NBEF) // 128
        xst = [xst0, xst1]

        @block.sync
        def _(sy):
            sy.dma_start(out=wr9s[:], in_=wr9_d.rearrange("(a p) e -> p a e", p=128)
                         ).then_inc(sems["xl"].h, 16)
            for st in range(NST):
                if st > 1:
                    sy.wait_ge(sems["tmm"].h, b2["tmm"] + 4 + 2 * (st - 2) + 1)
                base = st * 128
                if base < TM:
                    srcap = xT_part[:, base:base + 128]
                else:
                    srcap = xT_bef[:, base - TM:base - TM + 128]
                sy.dma_start(out=xst[st % 2][:],
                             in_=srcap.rearrange("(a p) t -> p a t", p=128)
                             ).then_inc(sems["xl"].h, 16)

        @block.gpsimd
        def _(g):
            for gc in range(NST // 4):
                if gc > 0:
                    g.wait_ge(sems["vve"].h, b2["vve"] + 2 * (4 * gc - 1) + 1)
                if gc < TM // 512:
                    vsrc = val_part[gc]
                else:
                    vsrc = val_bef[gc - TM // 512]
                g.dma_start(out=valc[:], in_=vsrc).then_inc(sems["vl"].h, 16)

        @block.tensor
        def _(t):
            for blk in range(8):
                mm = t.matmul(pC[:, 0:8], nd128[:], wrbf[:, blk, :],
                              start=(blk == 0), stop=(blk == 7))
            mm.then_inc(sems["tmm"].h, 1)
            t.nop().then_inc(sems["tmm"].h, 1)
            t.nop().then_inc(sems["tmm"].h, 1)
            bp = 4
            t.nop().then_inc(sems["tmm"].h, 1)
            for st in range(NST):
                t.wait_ge(sems["xl"].h, b2["xl"] + 16 * (st + 2))
                if st > 0:
                    t.wait_ge(sems["sev"].h, b2["sev"] + 3 + 2 * (st - 1) + 1)
                for blk in range(8):
                    mm = t.matmul(pA[:, 0:10], xst[st % 2][:, blk, :], wr9s[:, blk, :],
                                  start=(blk == 0), stop=(blk == 7))
                mm.then_inc(sems["tmm"].h, 1)
                t.wait_ge(sems["vve"].h, b2["vve"] + 2 * st + 1)
                if st > 0:
                    t.wait_ge(sems["sev"].h, b2["sev"] + 3 + 2 * (st - 1) + 2)
                mm = t.matmul(pD[0:1, 0:8], ones_col[:], mk1[:],
                              start=True, stop=True)
                mm.then_inc(sems["tmm"].h, 1)

        @block.scalar
        def _(s):
            s.wait_ge(sems["tmm"].h, b2["tmm"] + 1)
            s.copy(out=cs_b[:], in_=pC[:, 0:8]).then_inc(sems["sev"].h, 1)
            s.mul(sspc[:], epsc[:], 1.0).then_inc(sems["sev"].h, 1)
            s.mul(sspc[:], epsc[:], 1.0).then_inc(sems["sev"].h, 1)
            for st in range(NST):
                s.wait_ge(sems["tmm"].h, b2["tmm"] + 4 + 2 * st + 1)
                if st > 0:
                    s.wait_ge(sems["vve"].h, b2["vve"] + 2 * (st - 1) + 1)
                s.copy(out=t9s[:], in_=pA[:, 0:10]).then_inc(sems["sev"].h, 1)
                s.wait_ge(sems["tmm"].h, b2["tmm"] + 4 + 2 * st + 2)
                if st > 0:
                    s.wait_ge(sems["vve"].h, b2["vve"] + 2 * (st - 1) + 2)
                s.copy(out=trow[:], in_=pD[0:1, 0:8])
                s.mul(sspc[:], epsc[:], 1.0)
                s.mul(sspc[:], epsc[:], 1.0).then_inc(sems["sev"].h, 1)

        @block.vector
        def _(v):
            v.memset(totrow[:], 0.0)
            for st in range(NST):
                gc = st // 4
                v.wait_ge(sems["sev"].h, b2["sev"] + 3 + 2 * st + 1)
                v.wait_ge(sems["vl"].h, b2["vl"] + 16 * (gc + 1))
                if st == 8 and DBG:
                    v.tensor_copy(t9snap[:, 0:10], t9s[:])
                v.tensor_scalar(out=t8b[:], in0=cs_b[:], scalar1=t9s[:, 8:9],
                                scalar2=None, op0=OP.mult)
                v.tensor_copy(vspc[:], epsc[:])
                v.tensor_add(t8b[:], t8b[:], t9s[:, 0:8])
                v.tensor_copy(vspc[:], epsc[:])
                if st == 8 and DBG:
                    v.tensor_copy(t8pre[:], t8b[:])
                v.max_with_indices(out_max=m8a[:], out_indices=mi_u[:], in_=t8b[:])
                v.tensor_scalar(out=mk1[:], in0=t8b[:], scalar1=m8a[:, 0:1],
                                scalar2=None, op0=OP.is_equal)
                v.tensor_scalar(out=t8b[:], in0=t8b[:], scalar1=m8a[:, 1:2],
                                scalar2=None, op0=OP.is_equal)
                v.tensor_copy(vspc[:], epsc[:])
                v.tensor_add(mk1[:], mk1[:], t8b[:])
                v.tensor_copy(vspc[:], epsc[:])
                v.tensor_scalar(out=mk1[:], in0=mk1[:], scalar1=valc[:, st % 4:st % 4 + 1],
                                scalar2=None, op0=OP.mult).then_inc(sems["vve"].h, 1)
                v.wait_ge(sems["sev"].h, b2["sev"] + 3 + 2 * st + 2)
                if st == 8:
                    v.tensor_copy(t8snap[:], t8b[:])
                    v.tensor_copy(mksnap[:], mk1[:])
                v.tensor_add(totrow[:], totrow[:], trow[:]).then_inc(sems["vve"].h, 1)

        sems["xl"].inc(16 * (NST + 1))
        sems["vl"].inc(16 * (NST // 4))
        sems["sev"].inc(3 + 2 * NST)
        sems["vve"].inc(2 * NST)
        sems["tmm"].inc(4 + 2 * NST)
        barrier()

        b3p = snap()

        @block.scalar
        def _(s):
            s.copy(out=row1[0:1, 0:8], in_=totrow[:])
            s.mul(sspc[:], epsc[:], 1.0)
            s.mul(sspc[:], epsc[:], 1.0).then_inc(sems["sev"].h, 1)
            s.wait_ge(sems["tmm"].h, b3p["tmm"] + 1)
            s.copy(out=totB[:], in_=pT[0:8, 0:1])
            s.mul(sspc[:], epsc[:], 1.0)
            s.mul(sspc[:], epsc[:], 1.0).then_inc(sems["sev"].h, 1)
            for bb in range(4):
                s.wait_ge(sems["tmm"].h, b3p["tmm"] + 1 + bb + 1)
                s.copy(out=kq[:, bb * 8:(bb + 1) * 8, :],
                       in_=pT[:, 0:64].rearrange("p (q e) -> p q e", e=8)
                       ).then_inc(sems["sev"].h, 1)

        @block.tensor
        def _(t):
            t.wait_ge(sems["sev"].h, b3p["sev"] + 1)
            t.transpose(pT[0:8, 0:1], row1[0:1, 0:8], ident[0:1, 0:1]).then_inc(sems["tmm"].h, 1)
            t.wait_ge(sems["vve"].h, b3p["vve"] + 1)
            for bb in range(4):
                if bb > 0:
                    t.wait_ge(sems["sev"].h, b3p["sev"] + 2 + bb)
                for q in range(8):
                    T = bb * 8 + q
                    mm = t.transpose(pT[:, q * 8:(q + 1) * 8],
                                     keepT[:, T * 128:(T + 1) * 128], ident[0:8, 0:8])
                mm.then_inc(sems["tmm"].h, 1)

        @block.vector
        def _(v):
            v.wait_ge(sems["sev"].h, b3p["sev"] + 2)
            v.tensor_copy(keepT[:], maskT[:, 2048:6144])
            cur, oth = maskT, poscum
            for k in range(12):
                sh = 1 << k
                v.tensor_add(oth[:, 2048:6144], cur[:, 2048:6144],
                             cur[:, 2048 - sh:6144 - sh])
                v.tensor_copy(vspc[:], epsc[:])
                cur, oth = oth, cur
            v.tensor_scalar(out=cur[:, 2048:6144], in0=cur[:, 2048:6144],
                            scalar1=totB[:, 0:1], scalar2=None, op0=OP.add)
            v.tensor_copy(vspc[:], epsc[:])
            v.tensor_scalar(out=cur[:, 2048:6144], in0=cur[:, 2048:6144],
                            scalar1=float(CAP), scalar2=None, op0=OP.is_le)
            v.tensor_copy(vspc[:], epsc[:])
            v.tensor_mul(keepT[:], cur[:, 2048:6144], keepT[:]).then_inc(sems["vve"].h, 1)
            v.wait_ge(sems["sev"].h, b3p["sev"] + 6)
            v.tensor_mul(kq[:], kq[:], m1s[:])
            v.tensor_copy(vspc[:], epsc[:])
            for T in range(32):
                red = v.tensor_reduce(r32a[:, T:T + 1], kq[:, T, :], axis=AX.X, op=OP.add)
            red.then_inc(sems["vve"].h, 1)

        sems["sev"].inc(6)
        sems["tmm"].inc(5)
        sems["vve"].inc(2)
        barrier()

        b3q = snap()

        @block.tensor
        def _(t):
            for bb in range(4):
                if bb > 0:
                    t.wait_ge(sems["sev"].h, b3q["sev"] + bb)
                for q in range(8):
                    T = bb * 8 + q
                    mm = t.transpose(pT[:, q * 8:(q + 1) * 8],
                                     keepT[:, T * 128:(T + 1) * 128], ident[0:8, 0:8])
                mm.then_inc(sems["tmm"].h, 1)

        @block.scalar
        def _(s):
            for bb in range(4):
                s.wait_ge(sems["tmm"].h, b3q["tmm"] + bb + 1)
                s.copy(out=kq[:, bb * 8:(bb + 1) * 8, :],
                       in_=pT[:, 0:64].rearrange("p (q e) -> p q e", e=8)
                       ).then_inc(sems["sev"].h, 1)
            s.wait_ge(sems["vve"].h, b3q["vve"] + 1)
            s.activation(out=r32d[:], in_=r32c[:], func=AF.Ln)
            s.activation(out=r32d[:], in_=r32d[:], func=AF.Exp, scale=-1.0
                         ).then_inc(sems["sev"].h, 1)

        @block.vector
        def _(v):
            v.wait_ge(sems["sev"].h, b3q["sev"] + 4)
            v.tensor_mul(msum[:], kq[:], m2s[:])
            v.tensor_copy(vspc[:], epsc[:])
            for T in range(32):
                v.tensor_reduce(r32b[:, T:T + 1], msum[:, T, :], axis=AX.X, op=OP.add)
            v.tensor_mul(r32a[:], r32a[:], g12[:, :, 0])
            v.tensor_mul(r32b[:], r32b[:], g12[:, :, 1])
            v.tensor_copy(vspc[:], epsc[:])
            v.tensor_add(r32c[:], r32a[:], r32b[:])
            v.tensor_copy(vspc[:], epsc[:])
            v.tensor_scalar(out=r32c[:], in0=r32c[:], scalar1=1e-6, scalar2=None,
                            op0=OP.add).then_inc(sems["vve"].h, 1)
            v.wait_ge(sems["sev"].h, b3q["sev"] + 5)
            v.tensor_mul(r32a[:], r32a[:], r32d[:])
            v.tensor_mul(r32b[:], r32b[:], r32d[:])
            v.tensor_copy(vspc[:], epsc[:])
            for T in range(32):
                v.tensor_scalar(out=comb[:, T, :], in0=m1s[:, T, :],
                                scalar1=r32a[:, T:T + 1], scalar2=None, op0=OP.mult)
                v.tensor_scalar(out=msum[:, T, :], in0=m2s[:, T, :],
                                scalar1=r32b[:, T:T + 1], scalar2=None, op0=OP.mult)
            v.tensor_copy(vspc[:], epsc[:])
            v.tensor_add(comb[:], comb[:], msum[:]).then_inc(sems["vve"].h, 1)

        sems["tmm"].inc(4)
        sems["sev"].inc(5)
        sems["vve"].inc(2)
        barrier()

        if DBG:
            @block.sync
            def _(sy):
                n = 0
                for name, ap in [("d_g12", g12[:]), ("d_lgm", lgm[:]),
                                 ("d_m1s", m1s[:]), ("d_m2s", m2s[:]),
                                 ("d_kq", kq[:]), ("d_keep", keepT[:]),
                                 ("d_totB", totB[:]), ("d_mx", mx_all[:]),
                                 ("d_t9s", t9s[:]), ("d_mk1", mk1[:]),
                                 ("d_t8b", t8b[:]), ("d_totrow", totrow[:]),
                                 ("d_t8snap", t8snap[:]), ("d_mksnap", mksnap[:]),
                                 ("d_t9snap", t9snap[:, 0:10]), ("d_t8pre", t8pre[:]),
                                 ("d_csb", cs_b[:])]:
                    sy.dma_start(out=dbg_outs[name], in_=ap).then_inc(sems["xld"].h, 16)
                    n += 16
                sy.wait_ge(sems["xld"].h, sems["xld"].n + n)
            sems["xld"].inc(16 * 17)
            barrier()

    with nc.sbuf_tensor("uT", [128, 8, TM], BF16) as uT:
        wa_scope = ExitStack()
        Wa_sb = wa_scope.enter_context(
            nc.sbuf_tensor("Wa_sb", [128, 8, 1024], BF16))

        b4 = snap()
        NJ4 = L // 512

        @block.sync
        def _(sy):
            sy.dma_start(out=Wa_sb[:], in_=Wv_bf.rearrange("(a p) c -> p a c", p=128)
                         ).then_inc(sems["wl"].h, 16)

        @block.gpsimd
        def _(g):
            for j in range(NJ4):
                if j > 0:
                    g.wait_ge(sems["tmm"].h, b4["tmm"] + 10 * (j - 1) + 10)
                g.dma_start(out=xc[:], in_=xT_true[:, j * 512:(j + 1) * 512]
                            .rearrange("(a p) t -> p a t", p=128)
                            ).then_inc(sems["vl"].h, 16)
                for cb in range(8):
                    g.wait_ge(sems["sev"].h, b4["sev"] + 12 * j + 4 + cb + 1)
                    g.dma_start(out=vT_d[cb * 128:(cb + 1) * 128, j * 512:(j + 1) * 512],
                                in_=vstage[cb % 2][:]).then_inc(sems["st1"].h, 16)

        @block.scalar
        def _(s):
            for j in range(NJ4):
                s.wait_ge(sems["vl"].h, b4["vl"] + 16 * (j + 1))
                if j > 0:
                    s.wait_ge(sems["tmm"].h, b4["tmm"] + 10 * (j - 1) + 1)
                for blk in range(7):
                    s.activation(out=sq8[:, blk, :], in_=xc[:, blk, :], func=AF.Square)
                s.activation(out=sq8[:, 7, :], in_=xc[:, 7, :], func=AF.Square
                             ).then_inc(sems["sev"].h, 1)
                s.wait_ge(sems["tmm"].h, b4["tmm"] + 10 * j + 1)
                if j > 0:
                    s.wait_ge(sems["vve"].h, b4["vve"] + 3 * (j - 1) + 2)
                s.mul(row1[:], pC[0:1, :], 1.0 / D)
                s.mul(row2[:], pD[0:1, :], 1.0 / D).then_inc(sems["sev"].h, 1)
                s.wait_ge(sems["vve"].h, b4["vve"] + 3 * j + 1)
                s.activation(out=row3[:], in_=row3[:], func=AF.Ln, bias=epsc[0:1, 0:1],
                             scale=1.0)
                s.activation(out=row3[:], in_=row3[:], func=AF.Exp, scale=-0.5
                             ).then_inc(sems["sev"].h, 1)
                s.wait_ge(sems["tmm"].h, b4["tmm"] + 10 * j + 2)
                if j > 0:
                    s.wait_ge(sems["vve"].h, b4["vve"] + 3 * j)
                s.copy(out=rstd_b[:], in_=pT[:, 0:512])
                s.copy(out=nmr_b[:], in_=pT[:, 512:1024]).then_inc(sems["sev"].h, 1)
                for cb in range(8):
                    s.wait_ge(sems["tmm"].h, b4["tmm"] + 10 * j + 2 + cb + 1)
                    k = 8 * j + cb
                    if k > 1:
                        s.wait_ge(sems["st1"].h, b4["st1"] + 16 * (k - 1))
                    ps = pA if cb % 2 == 0 else pB
                    s.activation(out=vstage[cb % 2][:], in_=ps[:], func=AF.Identity,
                                 bias=bvc[:, cb:cb + 1]).then_inc(sems["sev"].h, 1)

        @block.vector
        def _(v):
            for j in range(NJ4):
                v.wait_ge(sems["sev"].h, b4["sev"] + 12 * j + 2)
                v.tensor_mul(row3[:], row1[:], row1[:])
                v.tensor_sub(row3[:], row2[:], row3[:]).then_inc(sems["vve"].h, 1)
                v.wait_ge(sems["sev"].h, b4["sev"] + 12 * j + 3)
                v.tensor_mul(row2[:], row1[:], row3[:])
                v.tensor_scalar(out=row2[:], in0=row2[:], scalar1=-1.0, scalar2=None,
                                op0=OP.mult).then_inc(sems["vve"].h, 1)
                v.wait_ge(sems["sev"].h, b4["sev"] + 12 * j + 4)
                for blk in range(8):
                    v.tensor_mul(xc[:, blk, :], xc[:, blk, :], rstd_b[:])
                    mm = v.tensor_add(xc[:, blk, :], xc[:, blk, :], nmr_b[:])
                mm.then_inc(sems["vve"].h, 1)

        @block.tensor
        def _(t):
            t.wait_ge(sems["wl"].h, b4["wl"] + 16)
            for j in range(NJ4):
                t.wait_ge(sems["sev"].h, b4["sev"] + 12 * j + 1)
                for blk in range(8):
                    t.matmul(pC[0:1, :], ones_col_bf[:], xc[:, blk, :],
                             start=(blk == 0), stop=(blk == 7))
                for blk in range(8):
                    mm = t.matmul(pD[0:1, :], ones_col_bf[:], sq8[:, blk, :],
                                  start=(blk == 0), stop=(blk == 7))
                mm.then_inc(sems["tmm"].h, 1)
                t.wait_ge(sems["vve"].h, b4["vve"] + 3 * j + 2)
                t.wait_ge(sems["sev"].h, b4["sev"] + 12 * j + 3)
                t.matmul(pT[:, 0:512], ones_row_f[0:1, 0:128], row3[:],
                         start=True, stop=True)
                t.matmul(pT[:, 512:1024], ones_row_f[0:1, 0:128], row2[:],
                         start=True, stop=True).then_inc(sems["tmm"].h, 1)
                t.wait_ge(sems["vve"].h, b4["vve"] + 3 * j + 3)
                for cb in range(8):
                    ps = pA if cb % 2 == 0 else pB
                    if cb >= 2:
                        t.wait_ge(sems["sev"].h, b4["sev"] + 12 * j + 4 + cb - 1)
                    elif j > 0:
                        t.wait_ge(sems["sev"].h, b4["sev"] + 12 * j)
                    for blk in range(8):
                        mm = t.matmul(ps[:], Wa_sb[:, blk, cb * 128:(cb + 1) * 128],
                                      xc[:, blk, :], start=(blk == 0), stop=(blk == 7))
                    mm.then_inc(sems["tmm"].h, 1)

        sems["wl"].inc(16)
        sems["vl"].inc(16 * NJ4)
        sems["sev"].inc(12 * NJ4)
        sems["vve"].inc(3 * NJ4)
        sems["tmm"].inc(10 * NJ4)
        sems["st1"].inc(16 * 8 * NJ4)
        barrier()

        b4u = snap()

        @block.sync
        def _(sy):
            sy.dma_start(out=Wa_sb[:], in_=Wu_bf.rearrange("(a p) c -> p a c", p=128)
                         ).then_inc(sems["wl"].h, 16)

        @block.tensor
        def _(t):
            t.wait_ge(sems["wl"].h, b4u["wl"] + 16)
            for k in range(64):
                mych, cb = divmod(k, 8)
                ps = pA if k % 2 == 0 else pB
                if k > 1:
                    t.wait_ge(sems["sev"].h, b4u["sev"] + k - 1)
                for blk in range(8):
                    mm = t.matmul(ps[:], Wa_sb[:, blk, cb * 128:(cb + 1) * 128],
                                  xnT[:, blk, mych * 512:(mych + 1) * 512],
                                  start=(blk == 0), stop=(blk == 7))
                mm.then_inc(sems["tmm"].h, 1)

        @block.scalar
        def _(s):
            for k in range(64):
                mych, cb = divmod(k, 8)
                s.wait_ge(sems["tmm"].h, b4u["tmm"] + k + 1)
                ps = pA if k % 2 == 0 else pB
                s.activation(out=uT[:, cb, mych * 512:(mych + 1) * 512], in_=ps[:],
                             func=AF.Identity, bias=buc[:, cb:cb + 1]
                             ).then_inc(sems["sev"].h, 1)

        sems["wl"].inc(16)
        sems["tmm"].inc(64)
        sems["sev"].inc(64)
        barrier()
        wa_scope.close()

        with nc.sbuf_tensor("f_pe", [64, 512], BF16) as f_pe, \
             nc.sbuf_tensor("f_x", [64, 512], F32) as f_x, \
             nc.sbuf_tensor("f_t1", [64, 512], F32) as f_t1, \
             nc.sbuf_tensor("f_t2", [64, 512], F32) as f_t2, \
             nc.sbuf_tensor("f_F1", [64, 512], BF16) as f_F1, \
             nc.sbuf_tensor("f_F2", [64, 512], BF16) as f_F2, \
             nc.sbuf_tensor("f_w", [64, 128], BF16) as f_w, \
             nc.sbuf_tensor("f_w3", [64, 1024], BF16) as f_w3, \
             nc.sbuf_tensor("f_b", [64, 2], F32) as f_b:

            b5 = snap()

            @block.gpsimd
            def _(g):
                g.dma_start(out=f_w[:, 0:64], in_=W1_bf).then_inc(sems["wlg"].h, 16)
                g.dma_start(out=f_w[:, 64:128], in_=W2_bf).then_inc(sems["wlg"].h, 16)
                g.dma_start(out=f_w3[:], in_=W3_bf).then_inc(sems["wlg"].h, 16)
                g.dma_start(out=f_b[:, 0:1], in_=b1c).then_inc(sems["wlg"].h, 16)
                g.dma_start(out=f_b[:, 1:2], in_=b2c).then_inc(sems["wlg"].h, 16)
                for j in range(16):
                    if j > 0:
                        g.wait_ge(sems["tmm"].h, b5["tmm"] + 10 * (j - 1) + 1)
                    g.dma_start(out=f_pe[:], in_=posT_d[:, j * 512:(j + 1) * 512]
                                ).then_inc(sems["vl"].h, 16)
                    g.wait_ge(sems["sev"].h, b5["sev"] + 1 + 12 * j + 12)
                    g.dma_start(out=hT_d[:, j * 512:(j + 1) * 512]
                                .rearrange("(a p) t -> p a t", p=128),
                                in_=xc[:]).then_inc(sems["st1"].h, 16)

            @block.scalar
            def _(s):
                s.wait_ge(sems["wlg"].h, b5["wlg"] + 5 * 16)
                s.mul(f_w[:, 64:128], f_w[:, 64:128], 0.5)
                s.mul(f_w3[:], f_w3[:], 0.5).then_inc(sems["sev"].h, 1)
                for j in range(16):
                    s.wait_ge(sems["tmm"].h, b5["tmm"] + 10 * j + 1)
                    if j > 0:
                        s.wait_ge(sems["vve"].h, b5["vve"] + 4 * (j - 1) + 4)
                    s.activation(out=f_x[:], in_=pA[0:64, :], func=AF.Identity,
                                 bias=f_b[:, 0:1])
                    s.activation(out=f_t1[:], in_=f_x[:], func=AF.Square
                                 ).then_inc(sems["sev"].h, 1)
                    s.wait_ge(sems["vve"].h, b5["vve"] + 4 * j + 1)
                    s.activation(out=f_t1[:], in_=f_t2[:], func=AF.Tanh,
                                 scale=GELU_C).then_inc(sems["sev"].h, 1)
                    s.wait_ge(sems["tmm"].h, b5["tmm"] + 10 * j + 2)
                    s.wait_ge(sems["vve"].h, b5["vve"] + 4 * j + 2)
                    s.activation(out=f_x[:], in_=pA[0:64, :], func=AF.Identity,
                                 bias=f_b[:, 1:2])
                    s.activation(out=f_t1[:], in_=f_x[:], func=AF.Square
                                 ).then_inc(sems["sev"].h, 1)
                    s.wait_ge(sems["vve"].h, b5["vve"] + 4 * j + 3)
                    s.activation(out=f_t1[:], in_=f_t2[:], func=AF.Tanh,
                                 scale=GELU_C).then_inc(sems["sev"].h, 1)
                    for cb in range(8):
                        s.wait_ge(sems["tmm"].h, b5["tmm"] + 10 * j + 2 + cb + 1)
                        if cb == 0 and j > 0:
                            s.wait_ge(sems["st1"].h, b5["st1"] + 16 * j)
                        ps = [pA, pB, pC, pD][cb % 4]
                        s.activation(out=xc[:, cb, :], in_=ps[:], func=AF.Identity,
                                     bias=b3c[:, cb:cb + 1]).then_inc(sems["sev"].h, 1)

            @block.vector
            def _(v):
                for j in range(16):
                    for st in range(2):
                        v.wait_ge(sems["sev"].h, b5["sev"] + 1 + 12 * j + 2 * st + 1)
                        v.tensor_mul(f_t2[:], f_t1[:], f_x[:])
                        v.scalar_tensor_tensor(out=f_t2[:], in0=f_t2[:], scalar=0.044715,
                                               in1=f_x[:], op0=OP.mult, op1=OP.add
                                               ).then_inc(sems["vve"].h, 1)
                        v.wait_ge(sems["sev"].h, b5["sev"] + 1 + 12 * j + 2 * st + 2)
                        if st == 1 and j > 0:
                            v.wait_ge(sems["tmm"].h, b5["tmm"] + 10 * (j - 1) + 10)
                        dst = f_F1 if st == 0 else f_F2
                        v.scalar_tensor_tensor(out=dst[:], in0=f_t1[:], scalar=1.0,
                                               in1=f_x[:], op0=OP.add, op1=OP.mult
                                               ).then_inc(sems["vve"].h, 1)

            @block.tensor
            def _(t):
                t.wait_ge(sems["sev"].h, b5["sev"] + 1)
                for j in range(16):
                    t.wait_ge(sems["vl"].h, b5["vl"] + 16 * (j + 1))
                    if j > 0:
                        t.wait_ge(sems["sev"].h, b5["sev"] + 1 + 12 * (j - 1) + 4 + 5)
                    t.matmul(pA[0:64, :], f_w[:, 0:64], f_pe[:], start=True, stop=True
                             ).then_inc(sems["tmm"].h, 1)
                    t.wait_ge(sems["vve"].h, b5["vve"] + 4 * j + 2)
                    t.wait_ge(sems["sev"].h, b5["sev"] + 1 + 12 * j + 1)
                    t.matmul(pA[0:64, :], f_w[:, 64:128], f_F1[:], start=True, stop=True
                             ).then_inc(sems["tmm"].h, 1)
                    for cb in range(8):
                        ps = [pA, pB, pC, pD][cb % 4]
                        if cb == 0:
                            t.wait_ge(sems["vve"].h, b5["vve"] + 4 * j + 4)
                            t.wait_ge(sems["sev"].h, b5["sev"] + 1 + 12 * j + 3)
                        if cb >= 4:
                            t.wait_ge(sems["sev"].h, b5["sev"] + 1 + 12 * j + 4 + cb - 3)
                        t.matmul(ps[:], f_w3[:, cb * 128:(cb + 1) * 128], f_F2[:],
                                 start=True, stop=True).then_inc(sems["tmm"].h, 1)

            sems["wlg"].inc(16 * 5)
            sems["vl"].inc(16 * 16)
            sems["sev"].inc(1 + 12 * 16)
            sems["vve"].inc(4 * 16)
            sems["tmm"].inc(10 * 16)
            sems["st1"].inc(16 * 16)
            barrier()

        with ExitStack() as c6:
            def c6t(name, shape=None, dt=BF16):
                return c6.enter_context(
                    nc.sbuf_tensor(name, shape or [128, 512], dt))
            Xh2 = [c6t(f"Xh{p}", [64, G, 128]) for p in range(2)]
            Xv2 = [c6t(f"Xv{p}", [64, G, 128]) for p in range(2)]
            Ur = c6t("Ur")
            Ui = c6t("Ui")
            Upr2 = [[c6t(f"Upr{k}{p}") for p in range(2)] for k in range(2)]
            Upi2 = [[c6t(f"Upi{k}{p}") for p in range(2)] for k in range(2)]
            Hr = c6t("Hr")
            Hi = c6t("Hi")
            Zr = c6t("Zr")
            Zi = c6t("Zi")
            Yr2 = [c6t(f"Yr{p}") for p in range(2)]
            Yi2 = [c6t(f"Yi{p}") for p in range(2)]
            Btr = c6t("Btr")
            Bti = c6t("Bti")
            Bpr2 = [c6t(f"Bpr{p}") for p in range(2)]
            Bpi2 = [c6t(f"Bpi{p}") for p in range(2)]
            finb2 = [c6t(f"finb{p}", [32, 512]) for p in range(2)]
            w1c = c6t("w1c")
            w2c = c6t("w2c")
            g1c = c6t("g1c")
            g2c = c6t("g2c")
            b6 = snap()
            NG6 = NSG
            NIT = NG6 + 3

            @block.sync
            def _(sy):
                for i in range(NG6):
                    if i >= 2:
                        sy.wait_ge(sems["mmA"].h, b6["mmA"] + 2 * (i - 2) + 2)
                    sy.dma_start(out=Xh2[i % 2][:, :, :],
                                 in_=hT_d[i * G:(i + 1) * G, :]
                                 .rearrange("c (a b) -> a c b", b=128)
                                 ).then_inc(sems["xld"].h, 16)
                    sy.dma_start(out=Xv2[i % 2][:, :, :],
                                 in_=vT_d[i * G:(i + 1) * G, :]
                                 .rearrange("c (a b) -> a c b", b=128)
                                 ).then_inc(sems["xld"].h, 16)

            @block.vector
            def _(v):
                for i in range(NIT):
                    if i < NG6:
                        v.wait_ge(sems["xld"].h, b6["xld"] + 32 * i + 16)
                        v.tensor_mul(Xh2[i % 2][:, :, :].rearrange("a c b -> a (c b)"),
                                     Xh2[i % 2][:, :, :].rearrange("a c b -> a (c b)"),
                                     sb["decay4"][:]).then_inc(sems["twf"].h, 1)
                    for k in range(2 if i < NG6 else 0):
                        v.wait_ge(sems["evA"].h, b6["evA"] + 2 * i + k + 1)
                        if i >= 2:
                            v.wait_ge(sems["mmB"].h, b6["mmB"] + 2 * (i - 2) + k + 1)
                        Upr, Upi = Upr2[k][i % 2], Upi2[k][i % 2]
                        v.tensor_mul(w1c[:], Ur[:], sb["Twfr4"][:])
                        v.tensor_mul(w2c[:], Ui[:], sb["Twfi4"][:])
                        v.tensor_sub(Upr[:], w1c[:], w2c[:])
                        v.tensor_mul(w1c[:], Ur[:], sb["Twfi4"][:])
                        v.tensor_mul(w2c[:], Ui[:], sb["Twfr4"][:])
                        v.tensor_add(Upi[:], w1c[:], w2c[:]).then_inc(sems["twf"].h, 1)
                    g2 = i - 2
                    if 0 <= g2 < NG6:
                        v.wait_ge(sems["evIA"].h, b6["evIA"] + g2 + 1)
                        if g2 >= 2:
                            v.wait_ge(sems["mmF"].h, b6["mmF"] + g2 - 1)
                        Bpr, Bpi = Bpr2[g2 % 2], Bpi2[g2 % 2]
                        v.tensor_mul(w1c[:], Btr[:], sb["Twbr4"][:])
                        v.tensor_mul(w2c[:], Bti[:], sb["Twbi4"][:])
                        v.tensor_sub(Bpr[:], w1c[:], w2c[:])
                        v.tensor_mul(w1c[:], Btr[:], sb["Twbi4"][:])
                        v.tensor_mul(w2c[:], Bti[:], sb["Twbr4"][:])
                        v.tensor_add(Bpi[:], w1c[:], w2c[:]).then_inc(sems["twi"].h, 1)

            @block.gpsimd
            def _(g_):
                for i in range(NIT):
                    g = i - 1
                    if 0 <= g < NG6:
                        g_.wait_ge(sems["evB"].h, b6["evB"] + 2 * g + 2)
                        if g >= 2:
                            g_.wait_ge(sems["mmIA"].h, b6["mmIA"] + g - 1)
                        Yr, Yi = Yr2[g % 2], Yi2[g % 2]
                        g_.tensor_mul(g1c[:], Zr[:], Hr[:])
                        g_.tensor_mul(g2c[:], Zi[:], Hi[:])
                        g_.tensor_sub(Yr[:], g1c[:], g2c[:])
                        g_.tensor_mul(g1c[:], Zr[:], Hi[:])
                        g_.tensor_mul(g2c[:], Zi[:], Hr[:])
                        g_.tensor_add(Yi[:], g1c[:], g2c[:]).then_inc(sems["pw"].h, 1)
                    g3 = i - 3
                    if 0 <= g3 < NG6:
                        g_.wait_ge(sems["evF"].h, b6["evF"] + g3 + 1)
                        g_.dma_start(out=vc_d[g3 * G:(g3 + 1) * G, :]
                                     .rearrange("c (a b) -> a c b", b=128),
                                     in_=finb2[g3 % 2][:, :]
                                     .rearrange("a (c b) -> a c b", b=128)
                                     ).then_inc(sems["st2"].h, 16)

            @block.tensor
            def _(t):
                for i in range(NIT):
                    g = i - 1
                    g2 = i - 2
                    g3 = i - 3
                    if i < NG6:
                        t.wait_ge(sems["xld"].h, b6["xld"] + 32 * i + 16)
                        t.wait_ge(sems["twf"].h, b6["twf"] + 3 * i + 1)
                        if i >= 1:
                            t.wait_ge(sems["evA"].h, b6["evA"] + 2 * (i - 1) + 2)
                        X = Xh2[i % 2]
                        for c in range(G):
                            t.matmul(pA[:, c * 128:(c + 1) * 128], X[:, c, :],
                                     sb["DrDi"][0:64, 0:128], start=True, stop=True)
                            mm = t.matmul(pB[:, c * 128:(c + 1) * 128], X[:, c, :],
                                          sb["DrDi"][0:64, 128:256], start=True,
                                          stop=True)
                        mm.then_inc(sems["mmA"].h, 1)
                    if 0 <= g < NG6:
                        t.wait_ge(sems["twf"].h, b6["twf"] + 3 * g + 2)
                        if g >= 1:
                            t.wait_ge(sems["evB"].h, b6["evB"] + 2 * (g - 1) + 2)
                        Upr, Upi = Upr2[0][g % 2], Upi2[0][g % 2]
                        t.matmul(pC[:], sb["Dr"][:], Upr[:], start=True, stop=False)
                        t.matmul(pD[:], sb["Dr"][:], Upi[:], start=True, stop=False)
                        t.matmul(pC[:], sb["negDi"][:], Upi[:], start=False, stop=True)
                        t.matmul(pD[:], sb["Di"][:], Upr[:], start=False, stop=True
                                 ).then_inc(sems["mmB"].h, 1)
                    if i < NG6:
                        t.wait_ge(sems["xld"].h, b6["xld"] + 32 * i + 32)
                        t.wait_ge(sems["evA"].h, b6["evA"] + 2 * i + 1)
                        X = Xv2[i % 2]
                        for c in range(G):
                            t.matmul(pA[:, c * 128:(c + 1) * 128], X[:, c, :],
                                     sb["DrDi"][0:64, 0:128], start=True, stop=True)
                            mm = t.matmul(pB[:, c * 128:(c + 1) * 128], X[:, c, :],
                                          sb["DrDi"][0:64, 128:256], start=True,
                                          stop=True)
                        mm.then_inc(sems["mmA"].h, 1)
                    if 0 <= g < NG6:
                        t.wait_ge(sems["twf"].h, b6["twf"] + 3 * g + 3)
                        t.wait_ge(sems["evB"].h, b6["evB"] + 2 * g + 1)
                        Upr, Upi = Upr2[1][g % 2], Upi2[1][g % 2]
                        t.matmul(pC[:], sb["Dr"][:], Upr[:], start=True, stop=False)
                        t.matmul(pD[:], sb["Dr"][:], Upi[:], start=True, stop=False)
                        t.matmul(pC[:], sb["negDi"][:], Upi[:], start=False, stop=True)
                        t.matmul(pD[:], sb["Di"][:], Upr[:], start=False, stop=True
                                 ).then_inc(sems["mmB"].h, 1)
                    if 0 <= g2 < NG6:
                        t.wait_ge(sems["pw"].h, b6["pw"] + g2 + 1)
                        if g2 >= 1:
                            t.wait_ge(sems["evIA"].h, b6["evIA"] + g2)
                        Yr, Yi = Yr2[g2 % 2], Yi2[g2 % 2]
                        for c in range(G):
                            sl = slice(c * 128, (c + 1) * 128)
                            t.matmul(pT[:, 0:512][:, sl], Yr[:, sl],
                                     sb["DbrDbi"][:, 0:128], start=True, stop=False)
                            t.matmul(pT[:, 512:1024][:, sl], Yr[:, sl],
                                     sb["DbrDbi"][:, 128:256], start=True, stop=False)
                            t.matmul(pT[:, 0:512][:, sl], Yi[:, sl],
                                     sb["negDbiDbr"][:, 0:128], start=False, stop=True)
                            mm = t.matmul(pT[:, 512:1024][:, sl], Yi[:, sl],
                                          sb["negDbiDbr"][:, 128:256], start=False,
                                          stop=True)
                        mm.then_inc(sems["mmIA"].h, 1)
                    if 0 <= g3 < NG6:
                        t.wait_ge(sems["twi"].h, b6["twi"] + g3 + 1)
                        if g3 >= 1:
                            t.wait_ge(sems["evF"].h, b6["evF"] + g3)
                        Bpr, Bpi = Bpr2[g3 % 2], Bpi2[g3 % 2]
                        t.matmul(pF[:], sb["DbrS"][:], Bpr[:], start=True, stop=False)
                        t.matmul(pF[:], sb["negDbiS"][:], Bpi[:], start=False,
                                 stop=True).then_inc(sems["mmF"].h, 1)

            @block.scalar
            def _(s):
                for i in range(NIT):
                    g = i - 1
                    g2 = i - 2
                    g3 = i - 3
                    if i < NG6:
                        s.wait_ge(sems["mmA"].h, b6["mmA"] + 2 * i + 1)
                        if i >= 1:
                            s.wait_ge(sems["twf"].h, b6["twf"] + 3 * (i - 1) + 3)
                        s.copy(out=Ur[:], in_=pA[:])
                        s.copy(out=Ui[:], in_=pB[:]).then_inc(sems["evA"].h, 1)
                    if 0 <= g < NG6:
                        s.wait_ge(sems["mmB"].h, b6["mmB"] + 2 * g + 1)
                        if g >= 1:
                            s.wait_ge(sems["pw"].h, b6["pw"] + g)
                        s.copy(out=Hr[:], in_=pC[:])
                        s.copy(out=Hi[:], in_=pD[:]).then_inc(sems["evB"].h, 1)
                    if i < NG6:
                        s.wait_ge(sems["mmA"].h, b6["mmA"] + 2 * i + 2)
                        s.wait_ge(sems["twf"].h, b6["twf"] + 3 * i + 2)
                        s.copy(out=Ur[:], in_=pA[:])
                        s.copy(out=Ui[:], in_=pB[:]).then_inc(sems["evA"].h, 1)
                    if 0 <= g < NG6:
                        s.wait_ge(sems["mmB"].h, b6["mmB"] + 2 * g + 2)
                        if g >= 1:
                            s.wait_ge(sems["pw"].h, b6["pw"] + g)
                        s.copy(out=Zr[:], in_=pC[:])
                        s.copy(out=Zi[:], in_=pD[:]).then_inc(sems["evB"].h, 1)
                    if 0 <= g2 < NG6:
                        s.wait_ge(sems["mmIA"].h, b6["mmIA"] + g2 + 1)
                        if g2 >= 1:
                            s.wait_ge(sems["twi"].h, b6["twi"] + g2)
                        s.copy(out=Btr[:], in_=pT[:, 0:512])
                        s.copy(out=Bti[:], in_=pT[:, 512:1024]
                               ).then_inc(sems["evIA"].h, 1)
                    if 0 <= g3 < NG6:
                        s.wait_ge(sems["mmF"].h, b6["mmF"] + g3 + 1)
                        if g3 >= 2:
                            s.wait_ge(sems["st2"].h, b6["st2"] + 16 * (g3 - 1))
                        s.copy(out=finb2[g3 % 2][:], in_=pF[:]
                               ).then_inc(sems["evF"].h, 1)

            sems["xld"].inc(32 * NG6)
            sems["mmA"].inc(2 * NG6)
            sems["mmB"].inc(2 * NG6)
            sems["mmIA"].inc(NG6)
            sems["mmF"].inc(NG6)
            sems["evA"].inc(2 * NG6)
            sems["evB"].inc(2 * NG6)
            sems["evIA"].inc(NG6)
            sems["evF"].inc(NG6)
            sems["twf"].inc(3 * NG6)
            sems["pw"].inc(NG6)
            sems["twi"].inc(NG6)
            sems["st2"].inc(16 * NG6)
            barrier()

        b7 = snap()

        @block.gpsimd
        def _(g):
            for cb in range(8):
                if cb > 0:
                    g.wait_ge(sems["vve"].h, b7["vve"] + cb)
                g.dma_start(out=xc[:].rearrange("p a t -> p (a t)"),
                            in_=vc_d[cb * 128:(cb + 1) * 128, :]
                            ).then_inc(sems["vl"].h, 16)

        @block.vector
        def _(v):
            for cb in range(8):
                v.wait_ge(sems["vl"].h, b7["vl"] + 16 * (cb + 1))
                v.tensor_mul(uT[:, cb, :], uT[:, cb, :],
                             xc[:].rearrange("p a t -> p (a t)")
                             ).then_inc(sems["vve"].h, 1)

        sems["vl"].inc(16 * 8)
        sems["vve"].inc(8)
        barrier()

        p8s = ExitStack()
        svst = [p8s.enter_context(nc.sbuf_tensor(f"svst{p}", [128, 512], F32))
                for p in range(2)]
        Wo_sb = p8s.enter_context(nc.sbuf_tensor("Wo_sb", [128, 8, 1024], BF16))
        b8 = snap()

        @block.sync
        def _(sy):
            sy.dma_start(out=Wo_sb[:], in_=Wo_bf.rearrange("(a p) c -> p a c", p=128)
                         ).then_inc(sems["wl"].h, 16)

        @block.gpsimd
        def _(g):
            for k in range(64):
                T, oc = divmod(k, 2)
                g.wait_ge(sems["sev"].h, b8["sev"] + k + 1)
                g.dma_start(out=s_d[T * 128:(T + 1) * 128, oc * 512:(oc + 1) * 512],
                            in_=svst[k % 2][:]).then_inc(sems["st3"].h, 16)

        @block.tensor
        def _(t):
            t.wait_ge(sems["wl"].h, b8["wl"] + 16)
            for k in range(64):
                T, oc = divmod(k, 2)
                ps = pA if k % 2 == 0 else pB
                if k > 1:
                    t.wait_ge(sems["sev"].h, b8["sev"] + k - 1)
                for cb in range(8):
                    t.matmul(ps[:], uT[:, cb, T * 128:(T + 1) * 128],
                             Wo_sb[:, cb, oc * 512:(oc + 1) * 512],
                             start=(cb == 0), stop=False)
                t.matmul(ps[:], ones_row_bf[0:1, 0:128],
                         bo_row[0:1, oc * 512:(oc + 1) * 512],
                         start=False, stop=True).then_inc(sems["tmm"].h, 1)

        @block.scalar
        def _(s):
            for k in range(64):
                s.wait_ge(sems["tmm"].h, b8["tmm"] + k + 1)
                if k > 1:
                    s.wait_ge(sems["st3"].h, b8["st3"] + 16 * (k - 1))
                ps = pA if k % 2 == 0 else pB
                s.copy(out=svst[k % 2][:], in_=ps[:]).then_inc(sems["sev"].h, 1)

        sems["wl"].inc(16)
        sems["tmm"].inc(64)
        sems["sev"].inc(64)
        sems["st3"].inc(16 * 64)
        p8s.close()
        barrier()

    poolA.close()

    with nc.sbuf_tensor("wg_sb", [128, 8, DHID], BF16) as wg_sb, \
         nc.sbuf_tensor("wd_sb0", [128, 16, 512], BF16) as wd_sb0, \
         nc.sbuf_tensor("wd_sb1", [128, 16, 512], BF16) as wd_sb1, \
         nc.sbuf_tensor("hidT", [128, 16, 512], BF16) as hidT, \
         nc.sbuf_tensor("acc", [128, 4, 1024], F32) as acc, \
         nc.sbuf_tensor("xtile9", [128, 1024], F32) as xtile, \
         nc.sbuf_tensor("stile9", [128, 1024], F32) as stile, \
         nc.sbuf_tensor("ev5129", [128, 512], F32) as ev512, \
         nc.sbuf_tensor("sg_t9", [128, 1024], F32) as sg_t, \
         nc.sbuf_tensor("sg_u9", [128, 1024], F32) as sg_u:

        b9 = snap()
        wd_sb = [wd_sb0, wd_sb1]
        NCC = TM // 512

        def down_tmm(kglob, tt):
            cc = kglob // 16
            kl = kglob % 16
            return b9["tmm"] + cc * 80 + 16 + kl * 4 + tt + 1

        @block.sync
        def _(sy):
            if DBG:
                sy.dma_start(out=dbg_outs["d_comb"], in_=comb[:]).then_inc(sems["xld"].h, 16)
            sy.dma_start(out=wg_sb[:], in_=wgT_bf.rearrange("(a p) c -> p a c", p=128)
                         ).then_inc(sems["wl"].h, 16)
            for cc in range(NCC):
                for e in range(NE):
                    for oc in range(2):
                        kglob = (cc * NE + e) * 2 + oc
                        if kglob > 1:
                            sy.wait_ge(sems["tmm"].h, down_tmm(kglob - 2, 3))
                        sy.dma_start(out=wd_sb[kglob % 2][:],
                                     in_=wdT_bf[e][:, oc * 512:(oc + 1) * 512]
                                     .rearrange("(a p) c -> p a c", p=128)
                                     ).then_inc(sems["wl"].h, 16)
                if DBG:
                    sy.wait_ge(sems["vve"].h, b9["vve"] + cc * 69 + 65)
                    sy.dma_start(out=dbg_outs["d_acc"][cc * 512:(cc + 1) * 512, :]
                                 .rearrange("(a p) d -> p a d", p=128),
                                 in_=acc[:]).then_inc(sems["xld"].h, 16)
                for tt in range(4):
                    T = cc * 4 + tt
                    if T > 0:
                        sy.wait_ge(sems["vve"].h,
                                   b9["vve"] + ((T - 1) // 4) * 69 + 65 + ((T - 1) % 4) + 1)
                    sy.dma_start(out=xtile[:], in_=x_mine[T * 128:(T + 1) * 128, :]
                                 ).then_inc(sems["xl"].h, 16)
                    sy.dma_start(out=stile[:], in_=s_d[T * 128:(T + 1) * 128, :]
                                 ).then_inc(sems["st5"].h, 16)
            sy.wait_ge(sems["st4"].h, b9["st4"] + 16 * 32)

        @block.gpsimd
        def _(g):
            for T in range(32):
                cc, tt = divmod(T, 4)
                g.wait_ge(sems["vve"].h, b9["vve"] + cc * 69 + 65 + tt + 1)
                g.dma_start(out=out_d[T * 128:(T + 1) * 128, :], in_=sg_t[:]
                            ).then_inc(sems["st4"].h, 16)

        @block.tensor
        def _(t):
            t.wait_ge(sems["wl"].h, b9["wl"] + 16)
            for cc in range(NCC):
                for hb in range(16):
                    ps = pC if hb % 2 == 0 else pD
                    if hb >= 2:
                        t.wait_ge(sems["sev"].h, b9["sev"] + cc * 84 + hb - 1)
                    elif cc > 0:
                        t.wait_ge(sems["sev"].h, b9["sev"] + (cc - 1) * 84 + 80)
                    for blk in range(8):
                        mm = t.matmul(ps[:], wg_sb[:, blk, hb * 128:(hb + 1) * 128],
                                      xnT[:, blk, cc * 512:(cc + 1) * 512],
                                      start=(blk == 0), stop=(blk == 7))
                    mm.then_inc(sems["tmm"].h, 1)
                for e in range(NE):
                    for oc in range(2):
                        kglob = (cc * NE + e) * 2 + oc
                        kl = (e * 2 + oc)
                        t.wait_ge(sems["wl"].h, b9["wl"] + 16 + 16 * (kglob + 1))
                        t.wait_ge(sems["sev"].h, b9["sev"] + cc * 84 + 16)
                        for tt in range(4):
                            ps = [pA, pB, pC, pD][tt]
                            if kl > 0:
                                t.wait_ge(sems["sev"].h,
                                          b9["sev"] + cc * 84 + 16 + (kl - 1) * 4 + tt + 1)
                            for hb in range(16):
                                mm = t.matmul(ps[:],
                                              hidT[:, hb, tt * 128:(tt + 1) * 128],
                                              wd_sb[kglob % 2][:, hb, :],
                                              start=(hb == 0), stop=(hb == 15))
                            mm.then_inc(sems["tmm"].h, 1)

        @block.scalar
        def _(s):
            for cc in range(NCC):
                for hb in range(16):
                    s.wait_ge(sems["tmm"].h, b9["tmm"] + cc * 80 + hb + 1)
                    ps = pC if hb % 2 == 0 else pD
                    s.activation(out=hidT[:, hb, :], in_=ps[:], func=AF.Silu
                                 ).then_inc(sems["sev"].h, 1)
                for kl in range(16):
                    for tt in range(4):
                        s.wait_ge(sems["tmm"].h, b9["tmm"] + cc * 80 + 16 + kl * 4 + tt + 1)
                        idx = kl * 4 + tt
                        if cc * 64 + idx > 0:
                            s.wait_ge(sems["vve"].h, b9["vve"] + cc * 69 + 1 + idx)
                        ps = [pA, pB, pC, pD][tt]
                        e = kl // 2
                        s.activation(out=ev512[:], in_=ps[:], func=AF.Identity,
                                     scale=comb[:, cc * 4 + tt, e:e + 1]
                                     ).then_inc(sems["sev"].h, 1)
                for tt in range(4):
                    T9 = cc * 4 + tt
                    s.wait_ge(sems["vve"].h, b9["vve"] + cc * 69 + 65)
                    if tt > 0:
                        s.wait_ge(sems["vve"].h, b9["vve"] + cc * 69 + 65 + tt)
                    s.activation(out=sg_u[:], in_=acc[:, tt, :], func=AF.Silu
                                 ).then_inc(sems["sev"].h, 1)

        @block.vector
        def _(v):
            for cc in range(NCC):
                if cc > 0:
                    v.wait_ge(sems["sev"].h, b9["sev"] + (cc - 1) * 84 + 84)
                    if DBG:
                        v.wait_ge(sems["xld"].h, b9["xld"] + 16 * (cc + 1))
                v.memset(acc[:], 0.0).then_inc(sems["vve"].h, 1)
                v.tensor_copy(vspc[:], epsc[:])
                for kl in range(16):
                    for tt in range(4):
                        idx = kl * 4 + tt
                        oc = kl % 2
                        v.wait_ge(sems["sev"].h, b9["sev"] + cc * 84 + 16 + idx + 1)
                        v.tensor_add(acc[:, tt, oc * 512:(oc + 1) * 512],
                                     acc[:, tt, oc * 512:(oc + 1) * 512],
                                     ev512[:]).then_inc(sems["vve"].h, 1)
                for tt in range(4):
                    T = cc * 4 + tt
                    v.wait_ge(sems["sev"].h, b9["sev"] + cc * 84 + 80 + tt + 1)
                    v.wait_ge(sems["st5"].h, b9["st5"] + 16 * (T + 1))
                    v.wait_ge(sems["xl"].h, b9["xl"] + 16 * (T + 1))
                    if T > 0:
                        v.wait_ge(sems["st4"].h, b9["st4"] + 16 * T)
                    v.tensor_mul(sg_t[:], stile[:], sg_u[:])
                    v.tensor_add(sg_t[:], sg_t[:], xtile[:]).then_inc(sems["vve"].h, 1)

        sems["wl"].inc(16 * (1 + NCC * NE * 2))
        if DBG:
            sems["xld"].inc(16 * (1 + NCC))
        sems["xl"].inc(16 * 32)
        sems["st5"].inc(16 * 32)
        sems["tmm"].inc(NCC * 80)
        sems["sev"].inc(NCC * 84)
        sems["vve"].inc(NCC * 69)
        sems["st4"].inc(16 * 32)
        barrier()

    if DBG:
        @block.sync
        def _(sy):
            n = 0
            for name, srcd in [("d_vT", vT_d), ("d_hT", hT_d), ("d_vc", vc_d),
                               ("d_s", s_d)]:
                sy.dma_start(out=dbg_outs[name], in_=srcd).then_inc(sems["ldc"].h, 16)
                n += 16
            sy.wait_ge(sems["ldc"].h, sems["ldc"].n + n)
        sems["ldc"].inc(64)
        barrier()

    ctx.close()
    build_program.last_sems = {k: v.n for k, v in sems.items()}
    return nc, dram_in


_CACHE = {}


def _get_program():
    if "nc" not in _CACHE:
        nc, dram_in = build_program()
        _CACHE["nc"] = nc
    return _CACHE["nc"]


def kernel(L0_attn_x, L0_attn_Wu, L0_attn_bu, L0_attn_Wv, L0_attn_bv,
           L0_attn_W1, L0_attn_b1, L0_attn_W2, L0_attn_b2, L0_attn_W3,
           L0_attn_b3, L0_attn_Wo, L0_attn_bo, L0_ffn_w_router,
           L0_ffn_w_gate, L0_ffn_w_down):
    x = _f32(L0_attn_x)
    shared = {
        "Wu": _f32(L0_attn_Wu), "Wv": _f32(L0_attn_Wv), "Wo": _f32(L0_attn_Wo),
        "W1": _f32(L0_attn_W1), "W2": _f32(L0_attn_W2), "W3": _f32(L0_attn_W3),
        "bu_c": _f32(np.asarray(L0_attn_bu).reshape(8, 128).T),
        "bv_c": _f32(np.asarray(L0_attn_bv).reshape(8, 128).T),
        "b3_c": _f32(np.asarray(L0_attn_b3).reshape(8, 128).T),
        "bo": _f32(L0_attn_bo).reshape(1, D),
        "b1c": _f32(np.asarray(L0_attn_b1).reshape(64, 1)),
        "b2c": _f32(np.asarray(L0_attn_b2).reshape(64, 1)),
        "wrT": _f32(np.ascontiguousarray(np.asarray(L0_ffn_w_router, np.float32).T)),
        "wr9": _f32(np.concatenate([np.asarray(L0_ffn_w_router, np.float32).T,
                                    np.ones((D, 1), np.float32),
                                    np.zeros((D, 1), np.float32)], 1)),
        "wgT": _f32(np.ascontiguousarray(np.asarray(L0_ffn_w_gate, np.float32).T)),
        "wdT": _f32(np.ascontiguousarray(
            np.asarray(L0_ffn_w_down, np.float32).transpose(0, 2, 1))),
    }
    in_maps = []
    for c in range(8):
        b, h = divmod(c, 2)
        consts = make_host_constants(h)
        m = dict(shared)
        m["x_mine"] = _f32(x[b, h * TM:(h + 1) * TM, :])
        m["xT_true"] = _f32(np.ascontiguousarray(x[b].T))
        m["xT_part"] = _f32(np.ascontiguousarray(x[b, (1 - h) * TM:(2 - h) * TM, :].T))
        xb = x[:b].reshape(-1, D)
        xbT = np.zeros((D, NBEF), np.float32)
        if b > 0:
            xbT[:, :xb.shape[0]] = xb.T
        m["xT_bef"] = xbT
        vb = np.zeros(NBEF, np.float32)
        vb[:b * L] = 1.0
        m["val_bef"] = _f32(vb.reshape(NBEF // 512, 4, 128).transpose(0, 2, 1))
        m["val_part"] = np.full((TM // 512, 128, 4), float(h), np.float32)
        for k, _shp in CONST_SPECS:
            m["c_" + k] = _bf(consts[k])
        m["c_posT"] = _f32(consts["posT"])
        m["c_ident"] = _f32(consts["ident"])
        m["c_iota8"] = _f32(consts["iota8"])
        m["c_ones_col"] = _f32(consts["ones_col"])
        m["c_ones_row"] = _f32(consts["ones_row"])
        m["c_epsc"] = np.full((128, 1), EPS, np.float32)
        m["c_ones8"] = np.ones((128, 8), np.float32)
        in_maps.append(m)

    nc = _get_program()
    res = run_bass_kernel_spmd(nc, in_maps, core_ids=list(range(8)))
    out = np.empty((B, L, D), np.float32)
    for c in range(8):
        b, h = divmod(c, 2)
        out[b, h * TM:(h + 1) * TM, :] = res.results[c]["out"]
    return out


# version 07 (final)
# speedup vs baseline: 1.0258x; 1.0258x over previous
import numpy as np
import ml_dtypes
import concourse.bass as bass
import concourse.mybir as mybir
from concourse.bass_utils import run_bass_kernel_spmd
from contextlib import ExitStack

F32 = mybir.dt.float32
BF16 = mybir.dt.bfloat16
U32 = mybir.dt.uint32
AF = mybir.ActivationFunctionType
OP = mybir.AluOpType
AX = mybir.AxisListType

B, L, D = 4, 8192, 1024
TM = 4096
NE, DHID = 8, 2048
CAP = 5120
NFFT = 16384
GELU_C = float(np.sqrt(2.0 / np.pi))
EPS = 1e-5
NBEF = 24576
G = 4
NSG = D // G

bfnp = ml_dtypes.bfloat16

CONST_SPECS = [("DrDi", [128, 256]), ("Dr", [128, 128]), ("Di", [128, 128]),
               ("negDi", [128, 128]), ("DbrDbi", [128, 256]),
               ("negDbiDbr", [128, 256]), ("DbrS", [128, 32]),
               ("negDbiS", [128, 32]),
               ("Twfr4", [128, 512]), ("Twfi4", [128, 512]),
               ("Twbr4", [128, 512]), ("Twbi4", [128, 512]),
               ("decay4", [64, 512])]


def _bf(x):
    return np.ascontiguousarray(np.asarray(x).astype(bfnp))


def _f32(x):
    return np.ascontiguousarray(np.asarray(x, dtype=np.float32))


def make_host_constants(myh):
    a = np.arange(128)
    Dm = np.exp(-2j * np.pi * np.outer(a, a) / 128)
    T = np.exp(-2j * np.pi * np.outer(a, a) / NFFT)
    Tb = np.exp(+2j * np.pi * np.outer(a, a) / NFFT) / NFFT
    Dr, Di = Dm.real, Dm.imag
    c = {}
    c["DrDi"] = np.concatenate([Dr, Di], 1)
    c["Dr"], c["Di"], c["negDi"] = Dr, Di, -Di
    c["DbrDbi"] = np.concatenate([Dr, -Di], 1)
    c["negDbiDbr"] = np.concatenate([Di, Dr], 1)
    c["DbrS"] = Dr[:, 32 * myh:32 * myh + 32]
    c["negDbiS"] = Di[:, 32 * myh:32 * myh + 32]
    c["Twfr4"] = np.concatenate([T.real] * G, 1)
    c["Twfi4"] = np.concatenate([T.imag] * G, 1)
    c["Twbr4"] = np.concatenate([Tb.real] * G, 1)
    c["Twbi4"] = np.concatenate([Tb.imag] * G, 1)
    t = np.arange(L, dtype=np.float64).reshape(64, 128)
    c["decay4"] = np.concatenate([np.exp(-0.01 * t)] * G, 1)
    pos = np.arange(L, dtype=np.float32)[:, None]
    div = np.exp(np.arange(0, 64, 2, dtype=np.float32) * -(np.log(10000.0) / 64))
    pe = np.zeros((L, 64), np.float32)
    pe[:, 0::2] = np.sin(pos * div)
    pe[:, 1::2] = np.cos(pos * div)
    c["posT"] = pe.T
    c["ident"] = np.eye(128, dtype=np.float32)
    c["iota8"] = np.tile(np.arange(8, dtype=np.float32), (128, 1))
    c["ones_col"] = np.ones((128, 1), np.float32)
    c["ones_row"] = np.ones((1, 512), np.float32)
    return c


class Ctr:
    def __init__(self, handle):
        self.h = handle
        self.n = 0

    def inc(self, k=1):
        self.n += k
        return self.n


def build_program():
    nc = bass.Bass("TRN2", target_bir_lowering=False)
    dram_in = {}

    def din(name, shape, dtype=F32):
        dram_in[name] = nc.dram_tensor(name, shape, dtype, kind="ExternalInput").ap()
        return dram_in[name]

    x_mine = din("x_mine", [TM, D])
    xT_true = din("xT_true", [D, L])
    xT_part = din("xT_part", [D, TM])
    xT_bef = din("xT_bef", [D, NBEF])
    val_bef = din("val_bef", [NBEF // 512, 128, 4])
    val_part = din("val_part", [TM // 512, 128, 4])
    Wu = din("Wu", [D, D]); Wv = din("Wv", [D, D]); Wo = din("Wo", [D, D])
    W1 = din("W1", [64, 64]); W2 = din("W2", [64, 64]); W3 = din("W3", [64, D])
    bu_c = din("bu_c", [128, 8]); bv_c = din("bv_c", [128, 8]); b3_c = din("b3_c", [128, 8])
    bo = din("bo", [1, D])
    b1c = din("b1c", [64, 1]); b2c = din("b2c", [64, 1])
    wrT = din("wrT", [D, NE])
    wr9_d = din("wr9", [D, NE + 2])
    wgT = din("wgT", [D, DHID])
    wdT = din("wdT", [NE, DHID, D])
    for k, shp in CONST_SPECS:
        din("c_" + k, shp, BF16)
    posT_d = din("c_posT", [64, L])
    ident_d = din("c_ident", [128, 128])
    iota8_d = din("c_iota8", [128, 8])
    ones_col_d = din("c_ones_col", [128, 1])
    ones_row_d = din("c_ones_row", [1, 512])
    ones8_d = din("c_ones8", [128, 8])
    epsc_d = din("c_epsc", [128, 1])

    out_d = nc.dram_tensor("out", [TM, D], F32, kind="ExternalOutput").ap()
    DBG = False
    if DBG:
        dbg_outs = {
            "d_vT": nc.dram_tensor("d_vT", [D, L], BF16, kind="ExternalOutput").ap(),
            "d_hT": nc.dram_tensor("d_hT", [D, L], BF16, kind="ExternalOutput").ap(),
            "d_vc": nc.dram_tensor("d_vc", [D, TM], BF16, kind="ExternalOutput").ap(),
            "d_s": nc.dram_tensor("d_s", [TM, D], F32, kind="ExternalOutput").ap(),
            "d_acc": nc.dram_tensor("d_acc", [TM, D], F32, kind="ExternalOutput").ap(),
            "d_comb": nc.dram_tensor("d_comb", [128, 32, 8], F32, kind="ExternalOutput").ap(),
            "d_g12": nc.dram_tensor("d_g12", [128, 32, 2], F32, kind="ExternalOutput").ap(),
            "d_lgm": nc.dram_tensor("d_lgm", [128, 32, 8], F32, kind="ExternalOutput").ap(),
            "d_m1s": nc.dram_tensor("d_m1s", [128, 32, 8], F32, kind="ExternalOutput").ap(),
            "d_m2s": nc.dram_tensor("d_m2s", [128, 32, 8], F32, kind="ExternalOutput").ap(),
            "d_kq": nc.dram_tensor("d_kq", [128, 32, 8], F32, kind="ExternalOutput").ap(),
            "d_keep": nc.dram_tensor("d_keep", [8, 4096], F32, kind="ExternalOutput").ap(),
            "d_totB": nc.dram_tensor("d_totB", [8, 1], F32, kind="ExternalOutput").ap(),
            "d_mx": nc.dram_tensor("d_mx", [128, 32, 8], F32, kind="ExternalOutput").ap(),
            "d_t9s": nc.dram_tensor("d_t9s", [128, 10], F32, kind="ExternalOutput").ap(),
            "d_mk1": nc.dram_tensor("d_mk1", [128, 8], F32, kind="ExternalOutput").ap(),
            "d_t8b": nc.dram_tensor("d_t8b", [128, 8], F32, kind="ExternalOutput").ap(),
            "d_totrow": nc.dram_tensor("d_totrow", [1, 8], F32, kind="ExternalOutput").ap(),
            "d_t8snap": nc.dram_tensor("d_t8snap", [128, 8], F32, kind="ExternalOutput").ap(),
            "d_mksnap": nc.dram_tensor("d_mksnap", [128, 8], F32, kind="ExternalOutput").ap(),
            "d_t9snap": nc.dram_tensor("d_t9snap", [128, 10], F32, kind="ExternalOutput").ap(),
            "d_t8pre": nc.dram_tensor("d_t8pre", [128, 8], F32, kind="ExternalOutput").ap(),
            "d_csb": nc.dram_tensor("d_csb", [128, 8], F32, kind="ExternalOutput").ap(),
        }

    def dint(name, shape, dtype):
        return nc.dram_tensor(name, shape, dtype, kind="Internal").ap()

    Wu_bf = dint("Wu_bf", [D, D], BF16)
    Wv_bf = dint("Wv_bf", [D, D], BF16)
    Wo_bf = dint("Wo_bf", [D, D], BF16)
    W1_bf = dint("W1_bf", [64, 64], BF16)
    W2_bf = dint("W2_bf", [64, 64], BF16)
    W3_bf = dint("W3_bf", [64, D], BF16)
    wgT_bf = dint("wgT_bf", [D, DHID], BF16)
    wdT_bf = dint("wdT_bf", [NE, DHID, D], BF16)
    vT_d = dint("vT_d", [D, L], BF16)
    hT_d = dint("hT_d", [D, L], BF16)
    vc_d = dint("vc_d", [D, TM], BF16)
    s_d = dint("s_d", [TM, D], F32)

    ctx = ExitStack()

    def sbuf(name, shape, dtype):
        return ctx.enter_context(nc.sbuf_tensor(name, shape, dtype))

    sb = {}
    for k, shp in CONST_SPECS:
        sb[k] = sbuf("k_" + k, shp, BF16)
    ident = sbuf("k_ident", [128, 128], F32)
    iota8 = sbuf("k_iota8", [128, 8], F32)
    ones_col_bf = sbuf("k_ones_col_bf", [128, 1], BF16)
    ones_row_bf = sbuf("k_ones_row_bf", [1, 512], BF16)
    ones_col = sbuf("k_ones_col", [128, 1], F32)
    ones8bf = sbuf("k_ones8bf", [128, 8], BF16)
    epsc = sbuf("k_epsc", [128, 1], F32)
    xnT = sbuf("xnT", [128, 8, TM], BF16)
    comb = sbuf("comb", [128, 32, 8], F32)
    buc = sbuf("buc", [128, 8], F32)
    bvc = sbuf("bvc", [128, 8], F32)
    b3c = sbuf("b3c", [128, 8], F32)
    wr32 = sbuf("wr32", [128, 8, 8], F32)
    wrbf = sbuf("wrbf", [128, 8, 8], BF16)
    t8a = sbuf("t8a", [128, 32], F32)
    mk4 = sbuf("mk4", [128, 4, 8], F32)
    m8a = sbuf("m8a", [128, 8], F32)
    t8b = sbuf("t8b", [128, 8], F32)
    c1 = sbuf("c1", [128, 1], F32)
    c2 = sbuf("c2", [128, 1], F32)
    c3 = sbuf("c3", [128, 1], F32)
    c4 = sbuf("c4", [128, 1], F32)
    valc = sbuf("valc", [128, 4], F32)
    mi_u = sbuf("mi_u", [128, 8], U32)
    bn6 = sbuf("bn6", [128, 12], F32)
    mv2 = sbuf("mv2", [128, 2], F32)
    vspc = sbuf("vspc", [128, 1], F32)
    sspc = sbuf("sspc", [128, 1], F32)
    poolA = ExitStack()

    def sbufA(name, shape, dtype):
        return poolA.enter_context(nc.sbuf_tensor(name, shape, dtype))

    xc = sbufA("xc", [128, 8, 512], BF16)
    sq8 = sbufA("sq8", [128, 8, 512], BF16)
    rstd_b = sbufA("rstd_b", [128, 512], F32)
    nmr_b = sbufA("nmr_b", [128, 512], F32)
    row1 = sbufA("row1", [1, 512], F32)
    row2 = sbufA("row2", [1, 512], F32)
    row3 = sbufA("row3", [1, 512], F32)
    lgT = sbufA("lgT", [8, 512], F32)
    csr_row = sbufA("csr_row", [1, 8], F32)
    ones_row_f = sbufA("ones_row_f", [1, 128], F32)
    vstage = [sbufA(f"vstage{p}", [128, 512], BF16) for p in range(2)]
    bo_row = sbufA("bo_row2", [1, D], BF16)

    pA = ctx.enter_context(nc.psum_tensor("pA", [128, 512], F32))
    pB = ctx.enter_context(nc.psum_tensor("pB", [128, 512], F32))
    pC = ctx.enter_context(nc.psum_tensor("pC", [128, 512], F32))
    pD = ctx.enter_context(nc.psum_tensor("pD", [128, 512], F32))
    pT = ctx.enter_context(nc.psum_tensor("pT", [128, 1024], F32))
    pF = ctx.enter_context(nc.psum_tensor("pF", [32, 512], F32))

    sems = {}
    for s in ["ldc", "cast", "xl", "vl", "wl",
              "mmA", "mmB", "mmIA", "mmF", "evA", "evB", "evIA", "evF",
              "twf", "pw", "twi", "xld",
              "tmm", "sev", "vve", "gps", "st1", "st2", "st3", "st4", "st5",
              "wlg", "ph"]:
        sems[s] = Ctr(ctx.enter_context(nc.semaphore("sm_" + s)))

    block = ctx.enter_context(nc.Block())
    nbar = [0]

    SYNC_DMA_SEMS = ["ldc", "xl", "wl", "st5", "xld"]
    GP_DMA_SEMS = ["cast", "vl", "wlg", "st1", "st2", "st3", "st4"]

    def barrier():
        nbar[0] += 1
        target = 5 * nbar[0]

        def tbody(e):
            e.nop().then_inc(sems["ph"].h, 1)
            e.wait_ge(sems["ph"].h, target)
        block.tensor(tbody)

        def sbody(e):
            e.drain()
            e.nop().then_inc(sems["ph"].h, 1)
            e.wait_ge(sems["ph"].h, target)
        block.scalar(sbody)

        def vbody(e):
            e.drain()
            e.nop().then_inc(sems["ph"].h, 1)
            e.wait_ge(sems["ph"].h, target)
        block.vector(vbody)

        def sybody(e):
            for s in SYNC_DMA_SEMS:
                if sems[s].n:
                    e.wait_ge(sems[s].h, sems[s].n)
            e.nop().then_inc(sems["ph"].h, 1)
            e.wait_ge(sems["ph"].h, target)
        block.sync(sybody)

        def gbody(e):
            for s in GP_DMA_SEMS:
                if sems[s].n:
                    e.wait_ge(sems[s].h, sems[s].n)
            e.nop().then_inc(sems["ph"].h, 1)
            e.wait_ge(sems["ph"].h, target)
        block.gpsimd(gbody)

    def snap():
        return {k: v.n for k, v in sems.items()}

    @block.sync
    def _(sy):
        n = 0
        for k, _shp in CONST_SPECS:
            sy.dma_start(out=sb[k][:], in_=dram_in["c_" + k]).then_inc(sems["ldc"].h, 16)
            n += 16
        for dst, src in [(ident, ident_d), (iota8, iota8_d), (ones_col, ones_col_d),
                         (epsc, epsc_d)]:
            sy.dma_start(out=dst[:], in_=src).then_inc(sems["ldc"].h, 16)
            n += 16
        for dst, src in [(buc, bu_c), (bvc, bv_c), (b3c, b3_c)]:
            sy.dma_start(out=dst[:], in_=src).then_inc(sems["ldc"].h, 16)
            n += 16
        sy.dma_start(out=wr32[:], in_=wrT.rearrange("(a p) e -> p a e", p=128)
                     ).then_inc(sems["ldc"].h, 16)
        n += 16
        sy.dma_start(out=ones_row_f[:], in_=ones_row_d[0:1, 0:128]).then_inc(sems["ldc"].h, 16)
        n += 16
        sy.wait_ge(sems["ldc"].h, n)
    sems["ldc"].inc(16 * (len(CONST_SPECS) + 9))

    @block.gpsimd
    def _(g):
        n = 0
        for src, dst in [(Wu, Wu_bf), (Wv, Wv_bf), (Wo, Wo_bf), (wgT, wgT_bf),
                         (W1, W1_bf), (W2, W2_bf), (W3, W3_bf)]:
            g.dma_start(out=dst, in_=src).then_inc(sems["cast"].h, 16)
            n += 16
        for e in range(NE):
            g.dma_start(out=wdT_bf[e], in_=wdT[e]).then_inc(sems["cast"].h, 16)
            n += 16
        for dst, src in [(ones_col_bf, ones_col_d), (ones_row_bf, ones_row_d),
                         (bo_row, bo)]:
            g.dma_start(out=dst[:], in_=src).then_inc(sems["cast"].h, 16)
            n += 16
        g.dma_start(out=wrbf[:], in_=wrT.rearrange("(a p) e -> p a e", p=128)
                    ).then_inc(sems["cast"].h, 16)
        n += 16
        g.dma_start(out=ones8bf[:], in_=ones8_d).then_inc(sems["cast"].h, 16)
        n += 16
        g.wait_ge(sems["cast"].h, n)
    sems["cast"].inc(16 * 20)

    barrier()

    with ExitStack() as c13:
        lgm = c13.enter_context(nc.sbuf_tensor("lgm", [128, 32, 8], F32))
        m1s = c13.enter_context(nc.sbuf_tensor("m1s", [128, 32, 8], F32))
        m2s = c13.enter_context(nc.sbuf_tensor("m2s", [128, 32, 8], F32))
        msum = c13.enter_context(nc.sbuf_tensor("msum", [128, 32, 8], F32))
        mx_all = c13.enter_context(nc.sbuf_tensor("mx_all", [128, 32, 8], F32))
        g12 = c13.enter_context(nc.sbuf_tensor("g12", [128, 32, 2], F32))
        r32a = c13.enter_context(nc.sbuf_tensor("r32a", [128, 32], F32))
        r32b = c13.enter_context(nc.sbuf_tensor("r32b", [128, 32], F32))
        r32c = c13.enter_context(nc.sbuf_tensor("r32c", [128, 32], F32))
        r32d = c13.enter_context(nc.sbuf_tensor("r32d", [128, 32], F32))
        maskT = c13.enter_context(nc.sbuf_tensor("maskT", [8, 6144], F32))
        poscum = c13.enter_context(nc.sbuf_tensor("poscum", [8, 6144], F32))
        keepT = c13.enter_context(nc.sbuf_tensor("keepT", [8, 4096], F32))
        totB = c13.enter_context(nc.sbuf_tensor("totB", [8, 1], F32))
        kq = c13.enter_context(nc.sbuf_tensor("kq", [128, 32, 8], F32))
        xtile = c13.enter_context(nc.sbuf_tensor("xtile", [128, 1024], F32))
        xn32 = c13.enter_context(nc.sbuf_tensor("xn32", [128, 1024], F32))
        xnT32 = c13.enter_context(nc.sbuf_tensor("xnT32", [128, 8, 128], F32))
        xst0 = c13.enter_context(nc.sbuf_tensor("xst0", [128, 8, 128], F32))
        xst1 = c13.enter_context(nc.sbuf_tensor("xst1", [128, 8, 128], F32))
        wr9s = c13.enter_context(nc.sbuf_tensor("wr9s", [128, 8, 10], F32))
        cs_b = c13.enter_context(nc.sbuf_tensor("cs_b", [128, 8], F32))
        mk1 = c13.enter_context(nc.sbuf_tensor("mk1", [128, 8], F32))
        t8snap = c13.enter_context(nc.sbuf_tensor("t8snap", [128, 8], F32))
        mksnap = c13.enter_context(nc.sbuf_tensor("mksnap", [128, 8], F32))
        t9snap = c13.enter_context(nc.sbuf_tensor("t9snap", [128, 16], F32))
        t8pre = c13.enter_context(nc.sbuf_tensor("t8pre", [128, 8], F32))
        nd128 = c13.enter_context(nc.sbuf_tensor("nd128", [128, 128], BF16))
        t9s = c13.enter_context(nc.sbuf_tensor("t9s", [128, 10], F32))
        totrow = c13.enter_context(nc.sbuf_tensor("totrow", [1, 8], F32))
        trow = c13.enter_context(nc.sbuf_tensor("trow", [1, 8], F32))

        b0 = snap()

        @block.vector
        def _(v):
            v.memset(maskT[:], 0.0)
            v.memset(poscum[:], 0.0)
            v.memset(nd128[:], -1.0 / D)
            for T in range(32):
                v.wait_ge(sems["xl"].h, b0["xl"] + 16 * (T + 1))
                if T > 0:
                    v.wait_ge(sems["sev"].h, b0["sev"] + 3 * (T - 1) + 1)
                v.bn_stats(bn6[:, 0:6], xtile[:, 0:512])
                v.bn_stats(bn6[:, 6:12], xtile[:, 512:1024])
                v.tensor_copy(vspc[:], epsc[:])
                v.tensor_copy(vspc[:], epsc[:])
                v.bn_aggr(mv2[:], bn6[:]).then_inc(sems["vve"].h, 1)

        @block.sync
        def _(sy):
            for T in range(32):
                if T > 0:
                    sy.wait_ge(sems["sev"].h, b0["sev"] + 3 * (T - 1) + 1)
                sy.dma_start(out=xtile[:], in_=x_mine[T * 128:(T + 1) * 128, :]
                             ).then_inc(sems["xl"].h, 16)

        @block.scalar
        def _(s):
            for T in range(32):
                s.wait_ge(sems["vve"].h, b0["vve"] + T + 1)
                s.activation(out=c1[:], in_=mv2[:, 1:2], func=AF.Ln, bias=epsc[:, 0:1], scale=1.0)
                s.activation(out=c2[:], in_=c1[:], func=AF.Exp, scale=-0.5)
                s.mul(c3[:], mv2[:, 0:1], -1.0)
                s.activation(out=c4[:], in_=c3[:], func=AF.Identity, scale=c2[:, 0:1])
                s.mul(sspc[:], epsc[:], 1.0)
                s.mul(sspc[:], epsc[:], 1.0)
                s.activation(out=xn32[:], in_=xtile[:], func=AF.Identity,
                             scale=c2[:, 0:1], bias=c4[:, 0:1]
                             ).then_inc(sems["sev"].h, 1)
                s.wait_ge(sems["tmm"].h, b0["tmm"] + 2 * T + 1)
                s.copy(out=xnT32[:], in_=pT[:].rearrange("p (a q) -> p a q", q=128))
                s.copy(out=xnT[:, :, T * 128:(T + 1) * 128],
                       in_=pT[:].rearrange("p (a q) -> p a q", q=128)
                       ).then_inc(sems["sev"].h, 1)
                s.wait_ge(sems["tmm"].h, b0["tmm"] + 2 * T + 2)
                s.copy(out=lgm[:, T, :], in_=pA[:, 0:8]).then_inc(sems["sev"].h, 1)

        @block.tensor
        def _(t):
            for T in range(32):
                t.wait_ge(sems["sev"].h, b0["sev"] + 3 * T + 1)
                for cb in range(8):
                    mm = t.transpose(pT[:, cb * 128:(cb + 1) * 128],
                                     xn32[:, cb * 128:(cb + 1) * 128], ident[:])
                mm.then_inc(sems["tmm"].h, 1)
                t.wait_ge(sems["sev"].h, b0["sev"] + 3 * T + 2)
                for cb in range(8):
                    mm = t.matmul(pA[:, 0:8], xnT32[:, cb, :], wr32[:, cb, :],
                                  start=(cb == 0), stop=(cb == 7))
                mm.then_inc(sems["tmm"].h, 1)

        sems["xl"].inc(16 * 32)
        sems["vve"].inc(32)
        sems["sev"].inc(3 * 32)
        sems["tmm"].inc(2 * 32)
        barrier()

        b1b = snap()

        @block.vector
        def _(v):
            for T in range(32):
                v.tensor_reduce(r32a[:, T:T + 1], lgm[:, T, :], axis=AX.X, op=OP.max)
            for T in range(32):
                v.tensor_scalar(out=lgm[:, T, :], in0=lgm[:, T, :],
                                scalar1=r32a[:, T:T + 1], scalar2=None, op0=OP.subtract)
            for T in range(32):
                v.max_with_indices(out_max=mx_all[:, T, :], out_indices=mi_u[:],
                                   in_=lgm[:, T, :])
                v.tensor_scalar(out=m1s[:, T, :], in0=lgm[:, T, :],
                                scalar1=mx_all[:, T, 0:1], scalar2=None,
                                op0=OP.is_equal)
                v.tensor_scalar(out=m2s[:, T, :], in0=lgm[:, T, :],
                                scalar1=mx_all[:, T, 1:2], scalar2=None,
                                op0=OP.is_equal)
            v.tensor_copy(vspc[:], epsc[:])
            v.tensor_copy(vspc[:], epsc[:])
            v.tensor_add(msum[:], m1s[:], m2s[:]).then_inc(sems["vve"].h, 1)
            v.wait_ge(sems["sev"].h, b1b["sev"] + 1)
            for T in range(32):
                red = v.tensor_reduce(r32b[:, T:T + 1], lgm[:, T, :], axis=AX.X,
                                      op=OP.add)
            red.then_inc(sems["vve"].h, 1)
            v.wait_ge(sems["sev"].h, b1b["sev"] + 2)
            v.tensor_mul(r32d[:], r32d[:], r32c[:])
            v.tensor_copy(g12[:, :, 0], r32c[:])
            v.tensor_copy(g12[:, :, 1], r32d[:]).then_inc(sems["vve"].h, 1)

        @block.scalar
        def _(s):
            s.wait_ge(sems["vve"].h, b1b["vve"] + 1)
            s.activation(out=lgm[:], in_=lgm[:], func=AF.Exp)
            for T in range(32):
                act = s.activation(out=r32d[:, T:T + 1], in_=mx_all[:, T, 1:2],
                                   func=AF.Exp)
            act.then_inc(sems["sev"].h, 1)
            s.wait_ge(sems["vve"].h, b1b["vve"] + 2)
            s.activation(out=r32c[:], in_=r32b[:], func=AF.Ln)
            s.activation(out=r32c[:], in_=r32c[:], func=AF.Exp, scale=-1.0
                         ).then_inc(sems["sev"].h, 1)
            for bb in range(4):
                s.wait_ge(sems["tmm"].h, b1b["tmm"] + bb + 1)
                s.copy(out=maskT[:, 2048 + bb * 1024:2048 + (bb + 1) * 1024],
                       in_=pT[0:8, :]).then_inc(sems["sev"].h, 1)

        @block.tensor
        def _(t):
            t.wait_ge(sems["vve"].h, b1b["vve"] + 1)
            for bb in range(4):
                if bb > 0:
                    t.wait_ge(sems["sev"].h, b1b["sev"] + 2 + bb)
                for q in range(8):
                    T = bb * 8 + q
                    mm = t.transpose(pT[0:8, q * 128:(q + 1) * 128], msum[:, T, :],
                                     ident[:])
                mm.then_inc(sems["tmm"].h, 1)

        sems["vve"].inc(3)
        sems["sev"].inc(6)
        sems["tmm"].inc(4)
        barrier()

        b2 = snap()
        NST = (TM + NBEF) // 128
        xst = [xst0, xst1]

        @block.sync
        def _(sy):
            sy.dma_start(out=wr9s[:], in_=wr9_d.rearrange("(a p) e -> p a e", p=128)
                         ).then_inc(sems["xl"].h, 16)
            for st in range(NST):
                if st > 1:
                    sy.wait_ge(sems["tmm"].h, b2["tmm"] + 4 + 2 * (st - 2) + 1)
                base = st * 128
                if base < TM:
                    srcap = xT_part[:, base:base + 128]
                else:
                    srcap = xT_bef[:, base - TM:base - TM + 128]
                sy.dma_start(out=xst[st % 2][:],
                             in_=srcap.rearrange("(a p) t -> p a t", p=128)
                             ).then_inc(sems["xl"].h, 16)

        @block.gpsimd
        def _(g):
            for gc in range(NST // 4):
                if gc > 0:
                    g.wait_ge(sems["vve"].h, b2["vve"] + 2 * (4 * gc - 1) + 1)
                if gc < TM // 512:
                    vsrc = val_part[gc]
                else:
                    vsrc = val_bef[gc - TM // 512]
                g.dma_start(out=valc[:], in_=vsrc).then_inc(sems["vl"].h, 16)

        @block.tensor
        def _(t):
            for blk in range(8):
                mm = t.matmul(pC[:, 0:8], nd128[:], wrbf[:, blk, :],
                              start=(blk == 0), stop=(blk == 7))
            mm.then_inc(sems["tmm"].h, 1)
            t.nop().then_inc(sems["tmm"].h, 1)
            t.nop().then_inc(sems["tmm"].h, 1)
            bp = 4
            t.nop().then_inc(sems["tmm"].h, 1)
            for st in range(NST):
                t.wait_ge(sems["xl"].h, b2["xl"] + 16 * (st + 2))
                if st > 0:
                    t.wait_ge(sems["sev"].h, b2["sev"] + 3 + 2 * (st - 1) + 1)
                for blk in range(8):
                    mm = t.matmul(pA[:, 0:10], xst[st % 2][:, blk, :], wr9s[:, blk, :],
                                  start=(blk == 0), stop=(blk == 7))
                mm.then_inc(sems["tmm"].h, 1)
                t.wait_ge(sems["vve"].h, b2["vve"] + 2 * st + 1)
                if st > 0:
                    t.wait_ge(sems["sev"].h, b2["sev"] + 3 + 2 * (st - 1) + 2)
                mm = t.matmul(pD[0:1, 0:8], ones_col[:], mk1[:],
                              start=True, stop=True)
                mm.then_inc(sems["tmm"].h, 1)

        @block.scalar
        def _(s):
            s.wait_ge(sems["tmm"].h, b2["tmm"] + 1)
            s.copy(out=cs_b[:], in_=pC[:, 0:8]).then_inc(sems["sev"].h, 1)
            s.mul(sspc[:], epsc[:], 1.0).then_inc(sems["sev"].h, 1)
            s.mul(sspc[:], epsc[:], 1.0).then_inc(sems["sev"].h, 1)
            for st in range(NST):
                s.wait_ge(sems["tmm"].h, b2["tmm"] + 4 + 2 * st + 1)
                if st > 0:
                    s.wait_ge(sems["vve"].h, b2["vve"] + 2 * (st - 1) + 1)
                s.copy(out=t9s[:], in_=pA[:, 0:10]).then_inc(sems["sev"].h, 1)
                s.wait_ge(sems["tmm"].h, b2["tmm"] + 4 + 2 * st + 2)
                if st > 0:
                    s.wait_ge(sems["vve"].h, b2["vve"] + 2 * (st - 1) + 2)
                s.copy(out=trow[:], in_=pD[0:1, 0:8])
                s.mul(sspc[:], epsc[:], 1.0)
                s.mul(sspc[:], epsc[:], 1.0).then_inc(sems["sev"].h, 1)

        @block.vector
        def _(v):
            v.memset(totrow[:], 0.0)
            for st in range(NST):
                gc = st // 4
                v.wait_ge(sems["sev"].h, b2["sev"] + 3 + 2 * st + 1)
                v.wait_ge(sems["vl"].h, b2["vl"] + 16 * (gc + 1))
                if st == 8 and DBG:
                    v.tensor_copy(t9snap[:, 0:10], t9s[:])
                v.tensor_scalar(out=t8b[:], in0=cs_b[:], scalar1=t9s[:, 8:9],
                                scalar2=None, op0=OP.mult)
                v.tensor_copy(vspc[:], epsc[:])
                v.tensor_add(t8b[:], t8b[:], t9s[:, 0:8])
                v.tensor_copy(vspc[:], epsc[:])
                if st == 8 and DBG:
                    v.tensor_copy(t8pre[:], t8b[:])
                v.max_with_indices(out_max=m8a[:], out_indices=mi_u[:], in_=t8b[:])
                v.tensor_scalar(out=mk1[:], in0=t8b[:], scalar1=m8a[:, 0:1],
                                scalar2=None, op0=OP.is_equal)
                v.tensor_scalar(out=t8b[:], in0=t8b[:], scalar1=m8a[:, 1:2],
                                scalar2=None, op0=OP.is_equal)
                v.tensor_copy(vspc[:], epsc[:])
                v.tensor_add(mk1[:], mk1[:], t8b[:])
                v.tensor_copy(vspc[:], epsc[:])
                v.tensor_scalar(out=mk1[:], in0=mk1[:], scalar1=valc[:, st % 4:st % 4 + 1],
                                scalar2=None, op0=OP.mult).then_inc(sems["vve"].h, 1)
                v.wait_ge(sems["sev"].h, b2["sev"] + 3 + 2 * st + 2)
                if st == 8:
                    v.tensor_copy(t8snap[:], t8b[:])
                    v.tensor_copy(mksnap[:], mk1[:])
                v.tensor_add(totrow[:], totrow[:], trow[:]).then_inc(sems["vve"].h, 1)

        sems["xl"].inc(16 * (NST + 1))
        sems["vl"].inc(16 * (NST // 4))
        sems["sev"].inc(3 + 2 * NST)
        sems["vve"].inc(2 * NST)
        sems["tmm"].inc(4 + 2 * NST)
        barrier()

        b3p = snap()

        @block.scalar
        def _(s):
            s.copy(out=row1[0:1, 0:8], in_=totrow[:])
            s.mul(sspc[:], epsc[:], 1.0)
            s.mul(sspc[:], epsc[:], 1.0).then_inc(sems["sev"].h, 1)
            s.wait_ge(sems["tmm"].h, b3p["tmm"] + 1)
            s.copy(out=totB[:], in_=pT[0:8, 0:1])
            s.mul(sspc[:], epsc[:], 1.0)
            s.mul(sspc[:], epsc[:], 1.0).then_inc(sems["sev"].h, 1)
            for bb in range(4):
                s.wait_ge(sems["tmm"].h, b3p["tmm"] + 1 + bb + 1)
                s.copy(out=kq[:, bb * 8:(bb + 1) * 8, :],
                       in_=pT[:, 0:64].rearrange("p (q e) -> p q e", e=8)
                       ).then_inc(sems["sev"].h, 1)

        @block.tensor
        def _(t):
            t.wait_ge(sems["sev"].h, b3p["sev"] + 1)
            t.transpose(pT[0:8, 0:1], row1[0:1, 0:8], ident[0:1, 0:1]).then_inc(sems["tmm"].h, 1)
            t.wait_ge(sems["vve"].h, b3p["vve"] + 1)
            for bb in range(4):
                if bb > 0:
                    t.wait_ge(sems["sev"].h, b3p["sev"] + 2 + bb)
                for q in range(8):
                    T = bb * 8 + q
                    mm = t.transpose(pT[:, q * 8:(q + 1) * 8],
                                     keepT[:, T * 128:(T + 1) * 128], ident[0:8, 0:8])
                mm.then_inc(sems["tmm"].h, 1)

        @block.vector
        def _(v):
            v.wait_ge(sems["sev"].h, b3p["sev"] + 2)
            v.tensor_copy(keepT[:], maskT[:, 2048:6144])
            cur, oth = maskT, poscum
            for k in range(12):
                sh = 1 << k
                v.tensor_add(oth[:, 2048:6144], cur[:, 2048:6144],
                             cur[:, 2048 - sh:6144 - sh])
                v.tensor_copy(vspc[:], epsc[:])
                cur, oth = oth, cur
            v.tensor_scalar(out=cur[:, 2048:6144], in0=cur[:, 2048:6144],
                            scalar1=totB[:, 0:1], scalar2=None, op0=OP.add)
            v.tensor_copy(vspc[:], epsc[:])
            v.tensor_scalar(out=cur[:, 2048:6144], in0=cur[:, 2048:6144],
                            scalar1=float(CAP), scalar2=None, op0=OP.is_le)
            v.tensor_copy(vspc[:], epsc[:])
            v.tensor_mul(keepT[:], cur[:, 2048:6144], keepT[:]).then_inc(sems["vve"].h, 1)
            v.wait_ge(sems["sev"].h, b3p["sev"] + 6)
            v.tensor_mul(kq[:], kq[:], m1s[:])
            v.tensor_copy(vspc[:], epsc[:])
            for T in range(32):
                red = v.tensor_reduce(r32a[:, T:T + 1], kq[:, T, :], axis=AX.X, op=OP.add)
            red.then_inc(sems["vve"].h, 1)

        sems["sev"].inc(6)
        sems["tmm"].inc(5)
        sems["vve"].inc(2)
        barrier()

        b3q = snap()

        @block.tensor
        def _(t):
            for bb in range(4):
                if bb > 0:
                    t.wait_ge(sems["sev"].h, b3q["sev"] + bb)
                for q in range(8):
                    T = bb * 8 + q
                    mm = t.transpose(pT[:, q * 8:(q + 1) * 8],
                                     keepT[:, T * 128:(T + 1) * 128], ident[0:8, 0:8])
                mm.then_inc(sems["tmm"].h, 1)

        @block.scalar
        def _(s):
            for bb in range(4):
                s.wait_ge(sems["tmm"].h, b3q["tmm"] + bb + 1)
                s.copy(out=kq[:, bb * 8:(bb + 1) * 8, :],
                       in_=pT[:, 0:64].rearrange("p (q e) -> p q e", e=8)
                       ).then_inc(sems["sev"].h, 1)
            s.wait_ge(sems["vve"].h, b3q["vve"] + 1)
            s.activation(out=r32d[:], in_=r32c[:], func=AF.Ln)
            s.activation(out=r32d[:], in_=r32d[:], func=AF.Exp, scale=-1.0
                         ).then_inc(sems["sev"].h, 1)

        @block.vector
        def _(v):
            v.wait_ge(sems["sev"].h, b3q["sev"] + 4)
            v.tensor_mul(msum[:], kq[:], m2s[:])
            v.tensor_copy(vspc[:], epsc[:])
            for T in range(32):
                v.tensor_reduce(r32b[:, T:T + 1], msum[:, T, :], axis=AX.X, op=OP.add)
            v.tensor_mul(r32a[:], r32a[:], g12[:, :, 0])
            v.tensor_mul(r32b[:], r32b[:], g12[:, :, 1])
            v.tensor_copy(vspc[:], epsc[:])
            v.tensor_add(r32c[:], r32a[:], r32b[:])
            v.tensor_copy(vspc[:], epsc[:])
            v.tensor_scalar(out=r32c[:], in0=r32c[:], scalar1=1e-6, scalar2=None,
                            op0=OP.add).then_inc(sems["vve"].h, 1)
            v.wait_ge(sems["sev"].h, b3q["sev"] + 5)
            v.tensor_mul(r32a[:], r32a[:], r32d[:])
            v.tensor_mul(r32b[:], r32b[:], r32d[:])
            v.tensor_copy(vspc[:], epsc[:])
            for T in range(32):
                v.tensor_scalar(out=comb[:, T, :], in0=m1s[:, T, :],
                                scalar1=r32a[:, T:T + 1], scalar2=None, op0=OP.mult)
                v.tensor_scalar(out=msum[:, T, :], in0=m2s[:, T, :],
                                scalar1=r32b[:, T:T + 1], scalar2=None, op0=OP.mult)
            v.tensor_copy(vspc[:], epsc[:])
            v.tensor_add(comb[:], comb[:], msum[:]).then_inc(sems["vve"].h, 1)

        sems["tmm"].inc(4)
        sems["sev"].inc(5)
        sems["vve"].inc(2)
        barrier()

        if DBG:
            @block.sync
            def _(sy):
                n = 0
                for name, ap in [("d_g12", g12[:]), ("d_lgm", lgm[:]),
                                 ("d_m1s", m1s[:]), ("d_m2s", m2s[:]),
                                 ("d_kq", kq[:]), ("d_keep", keepT[:]),
                                 ("d_totB", totB[:]), ("d_mx", mx_all[:]),
                                 ("d_t9s", t9s[:]), ("d_mk1", mk1[:]),
                                 ("d_t8b", t8b[:]), ("d_totrow", totrow[:]),
                                 ("d_t8snap", t8snap[:]), ("d_mksnap", mksnap[:]),
                                 ("d_t9snap", t9snap[:, 0:10]), ("d_t8pre", t8pre[:]),
                                 ("d_csb", cs_b[:])]:
                    sy.dma_start(out=dbg_outs[name], in_=ap).then_inc(sems["xld"].h, 16)
                    n += 16
                sy.wait_ge(sems["xld"].h, sems["xld"].n + n)
            sems["xld"].inc(16 * 17)
            barrier()

    with nc.sbuf_tensor("uT", [128, 8, TM], BF16) as uT:
        wa_scope = ExitStack()
        Wa_sb = wa_scope.enter_context(
            nc.sbuf_tensor("Wa_sb", [128, 8, 1024], BF16))

        b4 = snap()
        NJ4 = L // 512

        @block.sync
        def _(sy):
            sy.dma_start(out=Wa_sb[:], in_=Wv_bf.rearrange("(a p) c -> p a c", p=128)
                         ).then_inc(sems["wl"].h, 16)

        @block.gpsimd
        def _(g):
            for j in range(NJ4):
                if j > 0:
                    g.wait_ge(sems["tmm"].h, b4["tmm"] + 10 * (j - 1) + 10)
                g.dma_start(out=xc[:], in_=xT_true[:, j * 512:(j + 1) * 512]
                            .rearrange("(a p) t -> p a t", p=128)
                            ).then_inc(sems["vl"].h, 16)
                for cb in range(8):
                    g.wait_ge(sems["sev"].h, b4["sev"] + 12 * j + 4 + cb + 1)
                    g.dma_start(out=vT_d[cb * 128:(cb + 1) * 128, j * 512:(j + 1) * 512],
                                in_=vstage[cb % 2][:]).then_inc(sems["st1"].h, 16)

        @block.scalar
        def _(s):
            for j in range(NJ4):
                s.wait_ge(sems["vl"].h, b4["vl"] + 16 * (j + 1))
                if j > 0:
                    s.wait_ge(sems["tmm"].h, b4["tmm"] + 10 * (j - 1) + 1)
                for blk in range(7):
                    s.activation(out=sq8[:, blk, :], in_=xc[:, blk, :], func=AF.Square)
                s.activation(out=sq8[:, 7, :], in_=xc[:, 7, :], func=AF.Square
                             ).then_inc(sems["sev"].h, 1)
                s.wait_ge(sems["tmm"].h, b4["tmm"] + 10 * j + 1)
                if j > 0:
                    s.wait_ge(sems["vve"].h, b4["vve"] + 3 * (j - 1) + 2)
                s.mul(row1[:], pC[0:1, :], 1.0 / D)
                s.mul(row2[:], pD[0:1, :], 1.0 / D).then_inc(sems["sev"].h, 1)
                s.wait_ge(sems["vve"].h, b4["vve"] + 3 * j + 1)
                s.activation(out=row3[:], in_=row3[:], func=AF.Ln, bias=epsc[0:1, 0:1],
                             scale=1.0)
                s.activation(out=row3[:], in_=row3[:], func=AF.Exp, scale=-0.5
                             ).then_inc(sems["sev"].h, 1)
                s.wait_ge(sems["tmm"].h, b4["tmm"] + 10 * j + 2)
                if j > 0:
                    s.wait_ge(sems["vve"].h, b4["vve"] + 3 * j)
                s.copy(out=rstd_b[:], in_=pT[:, 0:512])
                s.copy(out=nmr_b[:], in_=pT[:, 512:1024]).then_inc(sems["sev"].h, 1)
                for cb in range(8):
                    s.wait_ge(sems["tmm"].h, b4["tmm"] + 10 * j + 2 + cb + 1)
                    k = 8 * j + cb
                    if k > 1:
                        s.wait_ge(sems["st1"].h, b4["st1"] + 16 * (k - 1))
                    ps = pA if cb % 2 == 0 else pB
                    s.activation(out=vstage[cb % 2][:], in_=ps[:], func=AF.Identity,
                                 bias=bvc[:, cb:cb + 1]).then_inc(sems["sev"].h, 1)

        @block.vector
        def _(v):
            for j in range(NJ4):
                v.wait_ge(sems["sev"].h, b4["sev"] + 12 * j + 2)
                v.tensor_mul(row3[:], row1[:], row1[:])
                v.tensor_sub(row3[:], row2[:], row3[:]).then_inc(sems["vve"].h, 1)
                v.wait_ge(sems["sev"].h, b4["sev"] + 12 * j + 3)
                v.tensor_mul(row2[:], row1[:], row3[:])
                v.tensor_scalar(out=row2[:], in0=row2[:], scalar1=-1.0, scalar2=None,
                                op0=OP.mult).then_inc(sems["vve"].h, 1)
                v.wait_ge(sems["sev"].h, b4["sev"] + 12 * j + 4)
                for blk in range(8):
                    v.tensor_mul(xc[:, blk, :], xc[:, blk, :], rstd_b[:])
                    mm = v.tensor_add(xc[:, blk, :], xc[:, blk, :], nmr_b[:])
                mm.then_inc(sems["vve"].h, 1)

        @block.tensor
        def _(t):
            t.wait_ge(sems["wl"].h, b4["wl"] + 16)
            for j in range(NJ4):
                t.wait_ge(sems["sev"].h, b4["sev"] + 12 * j + 1)
                for blk in range(8):
                    t.matmul(pC[0:1, :], ones_col_bf[:], xc[:, blk, :],
                             start=(blk == 0), stop=(blk == 7))
                for blk in range(8):
                    mm = t.matmul(pD[0:1, :], ones_col_bf[:], sq8[:, blk, :],
                                  start=(blk == 0), stop=(blk == 7))
                mm.then_inc(sems["tmm"].h, 1)
                t.wait_ge(sems["vve"].h, b4["vve"] + 3 * j + 2)
                t.wait_ge(sems["sev"].h, b4["sev"] + 12 * j + 3)
                t.matmul(pT[:, 0:512], ones_row_f[0:1, 0:128], row3[:],
                         start=True, stop=True)
                t.matmul(pT[:, 512:1024], ones_row_f[0:1, 0:128], row2[:],
                         start=True, stop=True).then_inc(sems["tmm"].h, 1)
                t.wait_ge(sems["vve"].h, b4["vve"] + 3 * j + 3)
                for cb in range(8):
                    ps = pA if cb % 2 == 0 else pB
                    if cb >= 2:
                        t.wait_ge(sems["sev"].h, b4["sev"] + 12 * j + 4 + cb - 1)
                    elif j > 0:
                        t.wait_ge(sems["sev"].h, b4["sev"] + 12 * j)
                    for blk in range(8):
                        mm = t.matmul(ps[:], Wa_sb[:, blk, cb * 128:(cb + 1) * 128],
                                      xc[:, blk, :], start=(blk == 0), stop=(blk == 7))
                    mm.then_inc(sems["tmm"].h, 1)

        sems["wl"].inc(16)
        sems["vl"].inc(16 * NJ4)
        sems["sev"].inc(12 * NJ4)
        sems["vve"].inc(3 * NJ4)
        sems["tmm"].inc(10 * NJ4)
        sems["st1"].inc(16 * 8 * NJ4)
        barrier()

        b4u = snap()

        @block.sync
        def _(sy):
            sy.dma_start(out=Wa_sb[:], in_=Wu_bf.rearrange("(a p) c -> p a c", p=128)
                         ).then_inc(sems["wl"].h, 16)

        @block.tensor
        def _(t):
            t.wait_ge(sems["wl"].h, b4u["wl"] + 16)
            for k in range(64):
                mych, cb = divmod(k, 8)
                ps = pA if k % 2 == 0 else pB
                if k > 1:
                    t.wait_ge(sems["sev"].h, b4u["sev"] + k - 1)
                for blk in range(8):
                    mm = t.matmul(ps[:], Wa_sb[:, blk, cb * 128:(cb + 1) * 128],
                                  xnT[:, blk, mych * 512:(mych + 1) * 512],
                                  start=(blk == 0), stop=(blk == 7))
                mm.then_inc(sems["tmm"].h, 1)

        @block.scalar
        def _(s):
            for k in range(64):
                mych, cb = divmod(k, 8)
                s.wait_ge(sems["tmm"].h, b4u["tmm"] + k + 1)
                ps = pA if k % 2 == 0 else pB
                s.activation(out=uT[:, cb, mych * 512:(mych + 1) * 512], in_=ps[:],
                             func=AF.Identity, bias=buc[:, cb:cb + 1]
                             ).then_inc(sems["sev"].h, 1)

        sems["wl"].inc(16)
        sems["tmm"].inc(64)
        sems["sev"].inc(64)
        barrier()
        wa_scope.close()

        with nc.sbuf_tensor("f_pe", [64, 512], BF16) as f_pe, \
             nc.sbuf_tensor("f_x", [64, 512], F32) as f_x, \
             nc.sbuf_tensor("f_t1", [64, 512], F32) as f_t1, \
             nc.sbuf_tensor("f_t2", [64, 512], F32) as f_t2, \
             nc.sbuf_tensor("f_F1", [64, 512], BF16) as f_F1, \
             nc.sbuf_tensor("f_F2", [64, 512], BF16) as f_F2, \
             nc.sbuf_tensor("f_w", [64, 128], BF16) as f_w, \
             nc.sbuf_tensor("f_w3", [64, 1024], BF16) as f_w3, \
             nc.sbuf_tensor("f_b", [64, 2], F32) as f_b:

            b5 = snap()

            @block.gpsimd
            def _(g):
                g.dma_start(out=f_w[:, 0:64], in_=W1_bf).then_inc(sems["wlg"].h, 16)
                g.dma_start(out=f_w[:, 64:128], in_=W2_bf).then_inc(sems["wlg"].h, 16)
                g.dma_start(out=f_w3[:], in_=W3_bf).then_inc(sems["wlg"].h, 16)
                g.dma_start(out=f_b[:, 0:1], in_=b1c).then_inc(sems["wlg"].h, 16)
                g.dma_start(out=f_b[:, 1:2], in_=b2c).then_inc(sems["wlg"].h, 16)
                for j in range(16):
                    if j > 0:
                        g.wait_ge(sems["tmm"].h, b5["tmm"] + 10 * (j - 1) + 1)
                    g.dma_start(out=f_pe[:], in_=posT_d[:, j * 512:(j + 1) * 512]
                                ).then_inc(sems["vl"].h, 16)
                    g.wait_ge(sems["sev"].h, b5["sev"] + 1 + 12 * j + 12)
                    g.dma_start(out=hT_d[:, j * 512:(j + 1) * 512]
                                .rearrange("(a p) t -> p a t", p=128),
                                in_=xc[:]).then_inc(sems["st1"].h, 16)

            @block.scalar
            def _(s):
                s.wait_ge(sems["wlg"].h, b5["wlg"] + 5 * 16)
                s.mul(f_w[:, 64:128], f_w[:, 64:128], 0.5)
                s.mul(f_w3[:], f_w3[:], 0.5).then_inc(sems["sev"].h, 1)
                for j in range(16):
                    s.wait_ge(sems["tmm"].h, b5["tmm"] + 10 * j + 1)
                    if j > 0:
                        s.wait_ge(sems["vve"].h, b5["vve"] + 4 * (j - 1) + 4)
                    s.activation(out=f_x[:], in_=pA[0:64, :], func=AF.Identity,
                                 bias=f_b[:, 0:1])
                    s.activation(out=f_t1[:], in_=f_x[:], func=AF.Square
                                 ).then_inc(sems["sev"].h, 1)
                    s.wait_ge(sems["vve"].h, b5["vve"] + 4 * j + 1)
                    s.activation(out=f_t1[:], in_=f_t2[:], func=AF.Tanh,
                                 scale=GELU_C).then_inc(sems["sev"].h, 1)
                    s.wait_ge(sems["tmm"].h, b5["tmm"] + 10 * j + 2)
                    s.wait_ge(sems["vve"].h, b5["vve"] + 4 * j + 2)
                    s.activation(out=f_x[:], in_=pA[0:64, :], func=AF.Identity,
                                 bias=f_b[:, 1:2])
                    s.activation(out=f_t1[:], in_=f_x[:], func=AF.Square
                                 ).then_inc(sems["sev"].h, 1)
                    s.wait_ge(sems["vve"].h, b5["vve"] + 4 * j + 3)
                    s.activation(out=f_t1[:], in_=f_t2[:], func=AF.Tanh,
                                 scale=GELU_C).then_inc(sems["sev"].h, 1)
                    for cb in range(8):
                        s.wait_ge(sems["tmm"].h, b5["tmm"] + 10 * j + 2 + cb + 1)
                        if cb == 0 and j > 0:
                            s.wait_ge(sems["st1"].h, b5["st1"] + 16 * j)
                        ps = [pA, pB, pC, pD][cb % 4]
                        s.activation(out=xc[:, cb, :], in_=ps[:], func=AF.Identity,
                                     bias=b3c[:, cb:cb + 1]).then_inc(sems["sev"].h, 1)

            @block.vector
            def _(v):
                for j in range(16):
                    for st in range(2):
                        v.wait_ge(sems["sev"].h, b5["sev"] + 1 + 12 * j + 2 * st + 1)
                        v.tensor_mul(f_t2[:], f_t1[:], f_x[:])
                        v.scalar_tensor_tensor(out=f_t2[:], in0=f_t2[:], scalar=0.044715,
                                               in1=f_x[:], op0=OP.mult, op1=OP.add
                                               ).then_inc(sems["vve"].h, 1)
                        v.wait_ge(sems["sev"].h, b5["sev"] + 1 + 12 * j + 2 * st + 2)
                        if st == 1 and j > 0:
                            v.wait_ge(sems["tmm"].h, b5["tmm"] + 10 * (j - 1) + 10)
                        dst = f_F1 if st == 0 else f_F2
                        v.scalar_tensor_tensor(out=dst[:], in0=f_t1[:], scalar=1.0,
                                               in1=f_x[:], op0=OP.add, op1=OP.mult
                                               ).then_inc(sems["vve"].h, 1)

            @block.tensor
            def _(t):
                t.wait_ge(sems["sev"].h, b5["sev"] + 1)
                for j in range(16):
                    t.wait_ge(sems["vl"].h, b5["vl"] + 16 * (j + 1))
                    if j > 0:
                        t.wait_ge(sems["sev"].h, b5["sev"] + 1 + 12 * (j - 1) + 4 + 5)
                    t.matmul(pA[0:64, :], f_w[:, 0:64], f_pe[:], start=True, stop=True
                             ).then_inc(sems["tmm"].h, 1)
                    t.wait_ge(sems["vve"].h, b5["vve"] + 4 * j + 2)
                    t.wait_ge(sems["sev"].h, b5["sev"] + 1 + 12 * j + 1)
                    t.matmul(pA[0:64, :], f_w[:, 64:128], f_F1[:], start=True, stop=True
                             ).then_inc(sems["tmm"].h, 1)
                    for cb in range(8):
                        ps = [pA, pB, pC, pD][cb % 4]
                        if cb == 0:
                            t.wait_ge(sems["vve"].h, b5["vve"] + 4 * j + 4)
                            t.wait_ge(sems["sev"].h, b5["sev"] + 1 + 12 * j + 3)
                        if cb >= 4:
                            t.wait_ge(sems["sev"].h, b5["sev"] + 1 + 12 * j + 4 + cb - 3)
                        t.matmul(ps[:], f_w3[:, cb * 128:(cb + 1) * 128], f_F2[:],
                                 start=True, stop=True).then_inc(sems["tmm"].h, 1)

            sems["wlg"].inc(16 * 5)
            sems["vl"].inc(16 * 16)
            sems["sev"].inc(1 + 12 * 16)
            sems["vve"].inc(4 * 16)
            sems["tmm"].inc(10 * 16)
            sems["st1"].inc(16 * 16)
            barrier()

        with ExitStack() as c6:
            def c6t(name, shape=None, dt=BF16):
                return c6.enter_context(
                    nc.sbuf_tensor(name, shape or [128, 512], dt))
            Xh2 = [c6t(f"Xh{p}", [64, G, 128]) for p in range(2)]
            Xv2 = [c6t(f"Xv{p}", [64, G, 128]) for p in range(2)]
            Ur2k = [c6t(f"Ur{k}") for k in range(2)]
            Ui2k = [c6t(f"Ui{k}") for k in range(2)]
            Upr2 = [[c6t(f"Upr{k}{p}") for p in range(2)] for k in range(2)]
            Upi2 = [[c6t(f"Upi{k}{p}") for p in range(2)] for k in range(2)]
            Hr = c6t("Hr")
            Hi = c6t("Hi")
            Zr = c6t("Zr")
            Zi = c6t("Zi")
            Yr2 = [c6t(f"Yr{p}") for p in range(2)]
            Yi2 = [c6t(f"Yi{p}") for p in range(2)]
            Btr = c6t("Btr")
            Bti = c6t("Bti")
            Bpr2 = [c6t(f"Bpr{p}") for p in range(2)]
            Bpi2 = [c6t(f"Bpi{p}") for p in range(2)]
            finb2 = [c6t(f"finb{p}", [32, 512]) for p in range(2)]
            w1c = c6t("w1c")
            w2c = c6t("w2c")
            g1c = c6t("g1c")
            g2c = c6t("g2c")
            b6 = snap()
            NG6 = NSG
            NIT = NG6 + 3

            @block.sync
            def _(sy):
                for i in range(NG6):
                    if i >= 2:
                        sy.wait_ge(sems["mmA"].h, b6["mmA"] + 2 * (i - 2) + 2)
                    sy.dma_start(out=Xh2[i % 2][:, :, :],
                                 in_=hT_d[i * G:(i + 1) * G, :]
                                 .rearrange("c (a b) -> a c b", b=128)
                                 ).then_inc(sems["xld"].h, 16)
                    sy.dma_start(out=Xv2[i % 2][:, :, :],
                                 in_=vT_d[i * G:(i + 1) * G, :]
                                 .rearrange("c (a b) -> a c b", b=128)
                                 ).then_inc(sems["xld"].h, 16)

            @block.vector
            def _(v):
                for i in range(NIT):
                    if i < NG6:
                        v.wait_ge(sems["xld"].h, b6["xld"] + 32 * i + 16)
                        v.tensor_mul(Xh2[i % 2][:, :, :].rearrange("a c b -> a (c b)"),
                                     Xh2[i % 2][:, :, :].rearrange("a c b -> a (c b)"),
                                     sb["decay4"][:]).then_inc(sems["twf"].h, 1)
                    for k in range(2 if i < NG6 else 0):
                        v.wait_ge(sems["evA"].h, b6["evA"] + 2 * i + k + 1)
                        if i >= 2:
                            v.wait_ge(sems["mmB"].h, b6["mmB"] + 2 * (i - 2) + k + 1)
                        Upr, Upi = Upr2[k][i % 2], Upi2[k][i % 2]
                        Ur, Ui = Ur2k[k], Ui2k[k]
                        v.tensor_mul(w1c[:], Ur[:], sb["Twfr4"][:])
                        v.tensor_mul(w2c[:], Ui[:], sb["Twfi4"][:])
                        v.tensor_sub(Upr[:], w1c[:], w2c[:])
                        v.tensor_mul(w1c[:], Ur[:], sb["Twfi4"][:])
                        v.tensor_mul(w2c[:], Ui[:], sb["Twfr4"][:])
                        v.tensor_add(Upi[:], w1c[:], w2c[:]).then_inc(sems["twf"].h, 1)
                    g2 = i - 2
                    if 0 <= g2 < NG6:
                        v.wait_ge(sems["evIA"].h, b6["evIA"] + g2 + 1)
                        if g2 >= 2:
                            v.wait_ge(sems["mmF"].h, b6["mmF"] + g2 - 1)
                        Bpr, Bpi = Bpr2[g2 % 2], Bpi2[g2 % 2]
                        v.tensor_mul(w1c[:], Btr[:], sb["Twbr4"][:])
                        v.tensor_mul(w2c[:], Bti[:], sb["Twbi4"][:])
                        v.tensor_sub(Bpr[:], w1c[:], w2c[:])
                        v.tensor_mul(w1c[:], Btr[:], sb["Twbi4"][:])
                        v.tensor_mul(w2c[:], Bti[:], sb["Twbr4"][:])
                        v.tensor_add(Bpi[:], w1c[:], w2c[:]).then_inc(sems["twi"].h, 1)

            @block.gpsimd
            def _(g_):
                for i in range(NIT):
                    g = i - 1
                    if 0 <= g < NG6:
                        g_.wait_ge(sems["evB"].h, b6["evB"] + 2 * g + 2)
                        if g >= 2:
                            g_.wait_ge(sems["mmIA"].h, b6["mmIA"] + g - 1)
                        Yr, Yi = Yr2[g % 2], Yi2[g % 2]
                        g_.tensor_mul(g1c[:], Zr[:], Hr[:])
                        g_.tensor_mul(g2c[:], Zi[:], Hi[:])
                        g_.tensor_sub(Yr[:], g1c[:], g2c[:])
                        g_.tensor_mul(g1c[:], Zr[:], Hi[:])
                        g_.tensor_mul(g2c[:], Zi[:], Hr[:])
                        g_.tensor_add(Yi[:], g1c[:], g2c[:]).then_inc(sems["pw"].h, 1)
                    g3 = i - 3
                    if 0 <= g3 < NG6:
                        g_.wait_ge(sems["evF"].h, b6["evF"] + g3 + 1)
                        g_.dma_start(out=vc_d[g3 * G:(g3 + 1) * G, :]
                                     .rearrange("c (a b) -> a c b", b=128),
                                     in_=finb2[g3 % 2][:, :]
                                     .rearrange("a (c b) -> a c b", b=128)
                                     ).then_inc(sems["st2"].h, 16)

            @block.tensor
            def _(t):
                for i in range(NIT):
                    g = i - 1
                    g2 = i - 2
                    g3 = i - 3
                    if i < NG6:
                        t.wait_ge(sems["xld"].h, b6["xld"] + 32 * i + 16)
                        t.wait_ge(sems["twf"].h, b6["twf"] + 3 * i + 1)
                        if i >= 1:
                            t.wait_ge(sems["evA"].h, b6["evA"] + 2 * (i - 1) + 2)
                        X = Xh2[i % 2]
                        for c in range(G):
                            t.matmul(pA[:, c * 128:(c + 1) * 128], X[:, c, :],
                                     sb["DrDi"][0:64, 0:128], start=True, stop=True)
                            mm = t.matmul(pB[:, c * 128:(c + 1) * 128], X[:, c, :],
                                          sb["DrDi"][0:64, 128:256], start=True,
                                          stop=True)
                        mm.then_inc(sems["mmA"].h, 1)
                    if 0 <= g < NG6:
                        t.wait_ge(sems["twf"].h, b6["twf"] + 3 * g + 2)
                        if g >= 1:
                            t.wait_ge(sems["evB"].h, b6["evB"] + 2 * (g - 1) + 2)
                        Upr, Upi = Upr2[0][g % 2], Upi2[0][g % 2]
                        t.matmul(pC[:], sb["Dr"][:], Upr[:], start=True, stop=False)
                        t.matmul(pD[:], sb["Dr"][:], Upi[:], start=True, stop=False)
                        t.matmul(pC[:], sb["negDi"][:], Upi[:], start=False, stop=True)
                        t.matmul(pD[:], sb["Di"][:], Upr[:], start=False, stop=True
                                 ).then_inc(sems["mmB"].h, 1)
                    if i < NG6:
                        t.wait_ge(sems["xld"].h, b6["xld"] + 32 * i + 32)
                        t.wait_ge(sems["evA"].h, b6["evA"] + 2 * i + 1)
                        X = Xv2[i % 2]
                        for c in range(G):
                            t.matmul(pA[:, c * 128:(c + 1) * 128], X[:, c, :],
                                     sb["DrDi"][0:64, 0:128], start=True, stop=True)
                            mm = t.matmul(pB[:, c * 128:(c + 1) * 128], X[:, c, :],
                                          sb["DrDi"][0:64, 128:256], start=True,
                                          stop=True)
                        mm.then_inc(sems["mmA"].h, 1)
                    if 0 <= g < NG6:
                        t.wait_ge(sems["twf"].h, b6["twf"] + 3 * g + 3)
                        t.wait_ge(sems["evB"].h, b6["evB"] + 2 * g + 1)
                        Upr, Upi = Upr2[1][g % 2], Upi2[1][g % 2]
                        t.matmul(pC[:], sb["Dr"][:], Upr[:], start=True, stop=False)
                        t.matmul(pD[:], sb["Dr"][:], Upi[:], start=True, stop=False)
                        t.matmul(pC[:], sb["negDi"][:], Upi[:], start=False, stop=True)
                        t.matmul(pD[:], sb["Di"][:], Upr[:], start=False, stop=True
                                 ).then_inc(sems["mmB"].h, 1)
                    if 0 <= g2 < NG6:
                        t.wait_ge(sems["pw"].h, b6["pw"] + g2 + 1)
                        if g2 >= 1:
                            t.wait_ge(sems["evIA"].h, b6["evIA"] + g2)
                        Yr, Yi = Yr2[g2 % 2], Yi2[g2 % 2]
                        for c in range(G):
                            sl = slice(c * 128, (c + 1) * 128)
                            t.matmul(pT[:, 0:512][:, sl], Yr[:, sl],
                                     sb["DbrDbi"][:, 0:128], start=True, stop=False)
                            t.matmul(pT[:, 512:1024][:, sl], Yr[:, sl],
                                     sb["DbrDbi"][:, 128:256], start=True, stop=False)
                            t.matmul(pT[:, 0:512][:, sl], Yi[:, sl],
                                     sb["negDbiDbr"][:, 0:128], start=False, stop=True)
                            mm = t.matmul(pT[:, 512:1024][:, sl], Yi[:, sl],
                                          sb["negDbiDbr"][:, 128:256], start=False,
                                          stop=True)
                        mm.then_inc(sems["mmIA"].h, 1)
                    if 0 <= g3 < NG6:
                        t.wait_ge(sems["twi"].h, b6["twi"] + g3 + 1)
                        if g3 >= 1:
                            t.wait_ge(sems["evF"].h, b6["evF"] + g3)
                        Bpr, Bpi = Bpr2[g3 % 2], Bpi2[g3 % 2]
                        t.matmul(pF[:], sb["DbrS"][:], Bpr[:], start=True, stop=False)
                        t.matmul(pF[:], sb["negDbiS"][:], Bpi[:], start=False,
                                 stop=True).then_inc(sems["mmF"].h, 1)

            @block.scalar
            def _(s):
                for i in range(NIT):
                    g = i - 1
                    g2 = i - 2
                    g3 = i - 3
                    if i < NG6:
                        s.wait_ge(sems["mmA"].h, b6["mmA"] + 2 * i + 1)
                        if i >= 1:
                            s.wait_ge(sems["twf"].h, b6["twf"] + 3 * (i - 1) + 2)
                        s.copy(out=Ur2k[0][:], in_=pA[:])
                        s.copy(out=Ui2k[0][:], in_=pB[:]).then_inc(sems["evA"].h, 1)
                    if 0 <= g < NG6:
                        s.wait_ge(sems["mmB"].h, b6["mmB"] + 2 * g + 1)
                        if g >= 1:
                            s.wait_ge(sems["pw"].h, b6["pw"] + g)
                        s.copy(out=Hr[:], in_=pC[:])
                        s.copy(out=Hi[:], in_=pD[:]).then_inc(sems["evB"].h, 1)
                    if i < NG6:
                        s.wait_ge(sems["mmA"].h, b6["mmA"] + 2 * i + 2)
                        if i >= 1:
                            s.wait_ge(sems["twf"].h, b6["twf"] + 3 * (i - 1) + 3)
                        s.copy(out=Ur2k[1][:], in_=pA[:])
                        s.copy(out=Ui2k[1][:], in_=pB[:]).then_inc(sems["evA"].h, 1)
                    if 0 <= g < NG6:
                        s.wait_ge(sems["mmB"].h, b6["mmB"] + 2 * g + 2)
                        if g >= 1:
                            s.wait_ge(sems["pw"].h, b6["pw"] + g)
                        s.copy(out=Zr[:], in_=pC[:])
                        s.copy(out=Zi[:], in_=pD[:]).then_inc(sems["evB"].h, 1)
                    if 0 <= g2 < NG6:
                        s.wait_ge(sems["mmIA"].h, b6["mmIA"] + g2 + 1)
                        if g2 >= 1:
                            s.wait_ge(sems["twi"].h, b6["twi"] + g2)
                        s.copy(out=Btr[:], in_=pT[:, 0:512])
                        s.copy(out=Bti[:], in_=pT[:, 512:1024]
                               ).then_inc(sems["evIA"].h, 1)
                    if 0 <= g3 < NG6:
                        s.wait_ge(sems["mmF"].h, b6["mmF"] + g3 + 1)
                        if g3 >= 2:
                            s.wait_ge(sems["st2"].h, b6["st2"] + 16 * (g3 - 1))
                        s.copy(out=finb2[g3 % 2][:], in_=pF[:]
                               ).then_inc(sems["evF"].h, 1)

            sems["xld"].inc(32 * NG6)
            sems["mmA"].inc(2 * NG6)
            sems["mmB"].inc(2 * NG6)
            sems["mmIA"].inc(NG6)
            sems["mmF"].inc(NG6)
            sems["evA"].inc(2 * NG6)
            sems["evB"].inc(2 * NG6)
            sems["evIA"].inc(NG6)
            sems["evF"].inc(NG6)
            sems["twf"].inc(3 * NG6)
            sems["pw"].inc(NG6)
            sems["twi"].inc(NG6)
            sems["st2"].inc(16 * NG6)
            barrier()

        b7 = snap()

        @block.gpsimd
        def _(g):
            for cb in range(8):
                if cb > 0:
                    g.wait_ge(sems["vve"].h, b7["vve"] + cb)
                g.dma_start(out=xc[:].rearrange("p a t -> p (a t)"),
                            in_=vc_d[cb * 128:(cb + 1) * 128, :]
                            ).then_inc(sems["vl"].h, 16)

        @block.vector
        def _(v):
            for cb in range(8):
                v.wait_ge(sems["vl"].h, b7["vl"] + 16 * (cb + 1))
                v.tensor_mul(uT[:, cb, :], uT[:, cb, :],
                             xc[:].rearrange("p a t -> p (a t)")
                             ).then_inc(sems["vve"].h, 1)

        sems["vl"].inc(16 * 8)
        sems["vve"].inc(8)
        barrier()

        p8s = ExitStack()
        svst = [p8s.enter_context(nc.sbuf_tensor(f"svst{p}", [128, 512], F32))
                for p in range(2)]
        Wo_sb = p8s.enter_context(nc.sbuf_tensor("Wo_sb", [128, 8, 1024], BF16))
        b8 = snap()

        @block.sync
        def _(sy):
            sy.dma_start(out=Wo_sb[:], in_=Wo_bf.rearrange("(a p) c -> p a c", p=128)
                         ).then_inc(sems["wl"].h, 16)

        @block.gpsimd
        def _(g):
            for k in range(64):
                T, oc = divmod(k, 2)
                g.wait_ge(sems["sev"].h, b8["sev"] + k + 1)
                g.dma_start(out=s_d[T * 128:(T + 1) * 128, oc * 512:(oc + 1) * 512],
                            in_=svst[k % 2][:]).then_inc(sems["st3"].h, 16)

        @block.tensor
        def _(t):
            t.wait_ge(sems["wl"].h, b8["wl"] + 16)
            for k in range(64):
                T, oc = divmod(k, 2)
                ps = pA if k % 2 == 0 else pB
                if k > 1:
                    t.wait_ge(sems["sev"].h, b8["sev"] + k - 1)
                for cb in range(8):
                    t.matmul(ps[:], uT[:, cb, T * 128:(T + 1) * 128],
                             Wo_sb[:, cb, oc * 512:(oc + 1) * 512],
                             start=(cb == 0), stop=False)
                t.matmul(ps[:], ones_row_bf[0:1, 0:128],
                         bo_row[0:1, oc * 512:(oc + 1) * 512],
                         start=False, stop=True).then_inc(sems["tmm"].h, 1)

        @block.scalar
        def _(s):
            for k in range(64):
                s.wait_ge(sems["tmm"].h, b8["tmm"] + k + 1)
                if k > 1:
                    s.wait_ge(sems["st3"].h, b8["st3"] + 16 * (k - 1))
                ps = pA if k % 2 == 0 else pB
                s.copy(out=svst[k % 2][:], in_=ps[:]).then_inc(sems["sev"].h, 1)

        sems["wl"].inc(16)
        sems["tmm"].inc(64)
        sems["sev"].inc(64)
        sems["st3"].inc(16 * 64)
        p8s.close()
        barrier()

    poolA.close()

    with nc.sbuf_tensor("wg_sb", [128, 8, DHID], BF16) as wg_sb, \
         nc.sbuf_tensor("wd_sb0", [128, 16, 512], BF16) as wd_sb0, \
         nc.sbuf_tensor("wd_sb1", [128, 16, 512], BF16) as wd_sb1, \
         nc.sbuf_tensor("hidT", [128, 16, 512], BF16) as hidT, \
         nc.sbuf_tensor("acc", [128, 4, 1024], F32) as acc, \
         nc.sbuf_tensor("xtile9", [128, 1024], F32) as xtile, \
         nc.sbuf_tensor("stile9", [128, 1024], F32) as stile, \
         nc.sbuf_tensor("ev5129", [128, 512], F32) as ev512, \
         nc.sbuf_tensor("sg_t9", [128, 1024], F32) as sg_t, \
         nc.sbuf_tensor("sg_u9", [128, 1024], F32) as sg_u:

        b9 = snap()
        wd_sb = [wd_sb0, wd_sb1]
        NCC = TM // 512

        def down_tmm(kglob, tt):
            cc = kglob // 16
            kl = kglob % 16
            return b9["tmm"] + cc * 80 + 16 + kl * 4 + tt + 1

        @block.sync
        def _(sy):
            if DBG:
                sy.dma_start(out=dbg_outs["d_comb"], in_=comb[:]).then_inc(sems["xld"].h, 16)
            sy.dma_start(out=wg_sb[:], in_=wgT_bf.rearrange("(a p) c -> p a c", p=128)
                         ).then_inc(sems["wl"].h, 16)
            for cc in range(NCC):
                for e in range(NE):
                    for oc in range(2):
                        kglob = (cc * NE + e) * 2 + oc
                        if kglob > 1:
                            sy.wait_ge(sems["tmm"].h, down_tmm(kglob - 2, 3))
                        sy.dma_start(out=wd_sb[kglob % 2][:],
                                     in_=wdT_bf[e][:, oc * 512:(oc + 1) * 512]
                                     .rearrange("(a p) c -> p a c", p=128)
                                     ).then_inc(sems["wl"].h, 16)
                if DBG:
                    sy.wait_ge(sems["vve"].h, b9["vve"] + cc * 69 + 65)
                    sy.dma_start(out=dbg_outs["d_acc"][cc * 512:(cc + 1) * 512, :]
                                 .rearrange("(a p) d -> p a d", p=128),
                                 in_=acc[:]).then_inc(sems["xld"].h, 16)
                for tt in range(4):
                    T = cc * 4 + tt
                    if T > 0:
                        sy.wait_ge(sems["vve"].h,
                                   b9["vve"] + ((T - 1) // 4) * 69 + 65 + ((T - 1) % 4) + 1)
                    sy.dma_start(out=xtile[:], in_=x_mine[T * 128:(T + 1) * 128, :]
                                 ).then_inc(sems["xl"].h, 16)
                    sy.dma_start(out=stile[:], in_=s_d[T * 128:(T + 1) * 128, :]
                                 ).then_inc(sems["st5"].h, 16)
            sy.wait_ge(sems["st4"].h, b9["st4"] + 16 * 32)

        @block.gpsimd
        def _(g):
            for T in range(32):
                cc, tt = divmod(T, 4)
                g.wait_ge(sems["vve"].h, b9["vve"] + cc * 69 + 65 + tt + 1)
                g.dma_start(out=out_d[T * 128:(T + 1) * 128, :], in_=sg_t[:]
                            ).then_inc(sems["st4"].h, 16)

        @block.tensor
        def _(t):
            t.wait_ge(sems["wl"].h, b9["wl"] + 16)
            for cc in range(NCC):
                for hb in range(16):
                    ps = pC if hb % 2 == 0 else pD
                    if hb >= 2:
                        t.wait_ge(sems["sev"].h, b9["sev"] + cc * 84 + hb - 1)
                    elif cc > 0:
                        t.wait_ge(sems["sev"].h, b9["sev"] + (cc - 1) * 84 + 80)
                    for blk in range(8):
                        mm = t.matmul(ps[:], wg_sb[:, blk, hb * 128:(hb + 1) * 128],
                                      xnT[:, blk, cc * 512:(cc + 1) * 512],
                                      start=(blk == 0), stop=(blk == 7))
                    mm.then_inc(sems["tmm"].h, 1)
                for e in range(NE):
                    for oc in range(2):
                        kglob = (cc * NE + e) * 2 + oc
                        kl = (e * 2 + oc)
                        t.wait_ge(sems["wl"].h, b9["wl"] + 16 + 16 * (kglob + 1))
                        t.wait_ge(sems["sev"].h, b9["sev"] + cc * 84 + 16)
                        for tt in range(4):
                            ps = [pA, pB, pC, pD][tt]
                            if kl > 0:
                                t.wait_ge(sems["sev"].h,
                                          b9["sev"] + cc * 84 + 16 + (kl - 1) * 4 + tt + 1)
                            for hb in range(16):
                                mm = t.matmul(ps[:],
                                              hidT[:, hb, tt * 128:(tt + 1) * 128],
                                              wd_sb[kglob % 2][:, hb, :],
                                              start=(hb == 0), stop=(hb == 15))
                            mm.then_inc(sems["tmm"].h, 1)

        @block.scalar
        def _(s):
            for cc in range(NCC):
                for hb in range(16):
                    s.wait_ge(sems["tmm"].h, b9["tmm"] + cc * 80 + hb + 1)
                    ps = pC if hb % 2 == 0 else pD
                    s.activation(out=hidT[:, hb, :], in_=ps[:], func=AF.Silu
                                 ).then_inc(sems["sev"].h, 1)
                for kl in range(16):
                    for tt in range(4):
                        s.wait_ge(sems["tmm"].h, b9["tmm"] + cc * 80 + 16 + kl * 4 + tt + 1)
                        idx = kl * 4 + tt
                        if cc * 64 + idx > 0:
                            s.wait_ge(sems["vve"].h, b9["vve"] + cc * 69 + 1 + idx)
                        ps = [pA, pB, pC, pD][tt]
                        e = kl // 2
                        s.activation(out=ev512[:], in_=ps[:], func=AF.Identity,
                                     scale=comb[:, cc * 4 + tt, e:e + 1]
                                     ).then_inc(sems["sev"].h, 1)
                for tt in range(4):
                    T9 = cc * 4 + tt
                    s.wait_ge(sems["vve"].h, b9["vve"] + cc * 69 + 65)
                    if tt > 0:
                        s.wait_ge(sems["vve"].h, b9["vve"] + cc * 69 + 65 + tt)
                    s.activation(out=sg_u[:], in_=acc[:, tt, :], func=AF.Silu
                                 ).then_inc(sems["sev"].h, 1)

        @block.vector
        def _(v):
            for cc in range(NCC):
                if cc > 0:
                    v.wait_ge(sems["sev"].h, b9["sev"] + (cc - 1) * 84 + 84)
                    if DBG:
                        v.wait_ge(sems["xld"].h, b9["xld"] + 16 * (cc + 1))
                v.memset(acc[:], 0.0).then_inc(sems["vve"].h, 1)
                v.tensor_copy(vspc[:], epsc[:])
                for kl in range(16):
                    for tt in range(4):
                        idx = kl * 4 + tt
                        oc = kl % 2
                        v.wait_ge(sems["sev"].h, b9["sev"] + cc * 84 + 16 + idx + 1)
                        v.tensor_add(acc[:, tt, oc * 512:(oc + 1) * 512],
                                     acc[:, tt, oc * 512:(oc + 1) * 512],
                                     ev512[:]).then_inc(sems["vve"].h, 1)
                for tt in range(4):
                    T = cc * 4 + tt
                    v.wait_ge(sems["sev"].h, b9["sev"] + cc * 84 + 80 + tt + 1)
                    v.wait_ge(sems["st5"].h, b9["st5"] + 16 * (T + 1))
                    v.wait_ge(sems["xl"].h, b9["xl"] + 16 * (T + 1))
                    if T > 0:
                        v.wait_ge(sems["st4"].h, b9["st4"] + 16 * T)
                    v.tensor_mul(sg_t[:], stile[:], sg_u[:])
                    v.tensor_add(sg_t[:], sg_t[:], xtile[:]).then_inc(sems["vve"].h, 1)

        sems["wl"].inc(16 * (1 + NCC * NE * 2))
        if DBG:
            sems["xld"].inc(16 * (1 + NCC))
        sems["xl"].inc(16 * 32)
        sems["st5"].inc(16 * 32)
        sems["tmm"].inc(NCC * 80)
        sems["sev"].inc(NCC * 84)
        sems["vve"].inc(NCC * 69)
        sems["st4"].inc(16 * 32)
        barrier()

    if DBG:
        @block.sync
        def _(sy):
            n = 0
            for name, srcd in [("d_vT", vT_d), ("d_hT", hT_d), ("d_vc", vc_d),
                               ("d_s", s_d)]:
                sy.dma_start(out=dbg_outs[name], in_=srcd).then_inc(sems["ldc"].h, 16)
                n += 16
            sy.wait_ge(sems["ldc"].h, sems["ldc"].n + n)
        sems["ldc"].inc(64)
        barrier()

    ctx.close()
    build_program.last_sems = {k: v.n for k, v in sems.items()}
    return nc, dram_in


_CACHE = {}


def _get_program():
    if "nc" not in _CACHE:
        nc, dram_in = build_program()
        _CACHE["nc"] = nc
    return _CACHE["nc"]


def kernel(L0_attn_x, L0_attn_Wu, L0_attn_bu, L0_attn_Wv, L0_attn_bv,
           L0_attn_W1, L0_attn_b1, L0_attn_W2, L0_attn_b2, L0_attn_W3,
           L0_attn_b3, L0_attn_Wo, L0_attn_bo, L0_ffn_w_router,
           L0_ffn_w_gate, L0_ffn_w_down):
    x = _f32(L0_attn_x)
    shared = {
        "Wu": _f32(L0_attn_Wu), "Wv": _f32(L0_attn_Wv), "Wo": _f32(L0_attn_Wo),
        "W1": _f32(L0_attn_W1), "W2": _f32(L0_attn_W2), "W3": _f32(L0_attn_W3),
        "bu_c": _f32(np.asarray(L0_attn_bu).reshape(8, 128).T),
        "bv_c": _f32(np.asarray(L0_attn_bv).reshape(8, 128).T),
        "b3_c": _f32(np.asarray(L0_attn_b3).reshape(8, 128).T),
        "bo": _f32(L0_attn_bo).reshape(1, D),
        "b1c": _f32(np.asarray(L0_attn_b1).reshape(64, 1)),
        "b2c": _f32(np.asarray(L0_attn_b2).reshape(64, 1)),
        "wrT": _f32(np.ascontiguousarray(np.asarray(L0_ffn_w_router, np.float32).T)),
        "wr9": _f32(np.concatenate([np.asarray(L0_ffn_w_router, np.float32).T,
                                    np.ones((D, 1), np.float32),
                                    np.zeros((D, 1), np.float32)], 1)),
        "wgT": _f32(np.ascontiguousarray(np.asarray(L0_ffn_w_gate, np.float32).T)),
        "wdT": _f32(np.ascontiguousarray(
            np.asarray(L0_ffn_w_down, np.float32).transpose(0, 2, 1))),
    }
    in_maps = []
    for c in range(8):
        b, h = divmod(c, 2)
        consts = make_host_constants(h)
        m = dict(shared)
        m["x_mine"] = _f32(x[b, h * TM:(h + 1) * TM, :])
        m["xT_true"] = _f32(np.ascontiguousarray(x[b].T))
        m["xT_part"] = _f32(np.ascontiguousarray(x[b, (1 - h) * TM:(2 - h) * TM, :].T))
        xb = x[:b].reshape(-1, D)
        xbT = np.zeros((D, NBEF), np.float32)
        if b > 0:
            xbT[:, :xb.shape[0]] = xb.T
        m["xT_bef"] = xbT
        vb = np.zeros(NBEF, np.float32)
        vb[:b * L] = 1.0
        m["val_bef"] = _f32(vb.reshape(NBEF // 512, 4, 128).transpose(0, 2, 1))
        m["val_part"] = np.full((TM // 512, 128, 4), float(h), np.float32)
        for k, _shp in CONST_SPECS:
            m["c_" + k] = _bf(consts[k])
        m["c_posT"] = _f32(consts["posT"])
        m["c_ident"] = _f32(consts["ident"])
        m["c_iota8"] = _f32(consts["iota8"])
        m["c_ones_col"] = _f32(consts["ones_col"])
        m["c_ones_row"] = _f32(consts["ones_row"])
        m["c_epsc"] = np.full((128, 1), EPS, np.float32)
        m["c_ones8"] = np.ones((128, 8), np.float32)
        in_maps.append(m)

    nc = _get_program()
    res = run_bass_kernel_spmd(nc, in_maps, core_ids=list(range(8)))
    out = np.empty((B, L, D), np.float32)
    for c in range(8):
        b, h = divmod(c, 2)
        out[b, h * TM:(h + 1) * TM, :] = res.results[c]["out"]
    return out
